# version 07
# baseline (speedup 1.0000x reference)
import numpy as np
import ml_dtypes
from contextlib import ExitStack
from dataclasses import dataclass

import concourse.bass as bass
import concourse.mybir as mybir
from concourse.bass_utils import run_bass_kernel_spmd

F32 = mybir.dt.float32
BF16 = mybir.dt.bfloat16
I32 = mybir.dt.int32
AF = mybir.ActivationFunctionType
ALU = mybir.AluOpType
AX = mybir.AxisListType

P = 128
BIGNEG = -1.0e9
LN_EPS = 1e-5
RMS_EPS = 1e-6


@dataclass
class Cfg:
    D: int = 2048
    INTER: int = 5504
    TC: int = 2048
    NH: int = 16
    TF: int = 512
    OSG: int = 1
    TG: int = 256
    QW: int = 512
    hgrp: int = 0

    @property
    def DC(self): return self.D // P
    @property
    def OCH(self): return self.INTER // P
    @property
    def NT(self): return self.TC // P
    @property
    def NHALF(self): return self.TC // self.TF
    @property
    def NFLG(self): return (self.OCH + self.OSG - 1) // self.OSG


class SemCtr:
    def __init__(self, nc, stack, name):
        self.h = stack.enter_context(nc.semaphore(name))
        self.n = 0

    def inc(self, instr, k=1):
        instr.then_inc(self.h, k)
        self.n += k
        return instr


def build_program(cfg: Cfg, phases=99, solo=False, detect_races=True, ffn_parts=99):
    c = cfg
    D, INTER, TC, NH = c.D, c.INTER, c.TC, c.NH
    DC, OCH, NT, TF, TG = c.DC, c.OCH, c.NT, c.TF, c.TG
    QW = min(c.QW, TC)
    NQG = TC // QW
    N512 = min(512, TC)
    NES = D // min(512, D)
    NDS = D // min(512, D)
    NGD = TG // P
    NTT = TF // P
    NTG = TF // TG
    NFLG = c.NFLG
    NRBF = (OCH + 1) // 2
    IHE = NRBF * P
    HB = 4
    HGRP = c.hgrp if c.hgrp else min(8, NH)
    NG = NH // HGRP
    NCHB = NH // HB
    GW = TF // 512 if TF >= 512 else 1
    GWW = min(512, TF)

    nc = bass.Bass("TRN2", target_bir_lowering=False,
                   detect_race_conditions=detect_races)
    ap = lambda t: t.ap()
    x_in = ap(nc.dram_tensor("x_in", [TC, D], F32, kind="ExternalInput"))
    wq_d = ap(nc.dram_tensor("wq", [D, D], F32, kind="ExternalInput"))
    wk_d = ap(nc.dram_tensor("wk", [D, D], F32, kind="ExternalInput"))
    wv_d = ap(nc.dram_tensor("wv", [D, D], F32, kind="ExternalInput"))
    wo_d = ap(nc.dram_tensor("wo", [D, D], F32, kind="ExternalInput"))
    wf_d = ap(nc.dram_tensor("wf", [D, NH], F32, kind="ExternalInput"))
    normw_d = ap(nc.dram_tensor("normw", [1, D], F32, kind="ExternalInput"))
    lut_d = ap(nc.dram_tensor("lutrow", [1, 16], F32, kind="ExternalInput"))
    mata_d = ap(nc.dram_tensor("mata_idx", [OCH * D, P], I32, kind="ExternalInput"))
    matb_d = ap(nc.dram_tensor("matb_idx", [NRBF * P, D], I32, kind="ExternalInput"))
    triu_d = ap(nc.dram_tensor("triu", [P, P], F32, kind="ExternalInput"))
    identb_d = ap(nc.dram_tensor("identb", [P, P], BF16, kind="ExternalInput"))
    identf_d = ap(nc.dram_tensor("identf", [P, P], F32, kind="ExternalInput"))
    uxh_d = ap(nc.dram_tensor("uxh", [P, HB * P], F32, kind="ExternalInput"))
    ones_d = ap(nc.dram_tensor("onesrow", [1, P], F32, kind="ExternalInput"))
    mask_d = ap(nc.dram_tensor("maskrow", [1, P], F32, kind="ExternalInput"))
    consts_d = ap(nc.dram_tensor("consts", [P, 8], F32, kind="ExternalInput"))
    parity_d = ap(nc.dram_tensor("parity", [1, 1], I32, kind="ExternalInput"))
    nonce_d = ap(nc.dram_tensor("nonce", [1, 1], I32, kind="ExternalInput"))

    y_out = ap(nc.dram_tensor("y_out", [TC, D], F32, kind="ExternalOutput"))

    x1_park = ap(nc.dram_tensor("x1_park", [TC, D], F32, kind="Internal"))
    x2_park = ap(nc.dram_tensor("x2_park", [TC, D], F32, kind="Internal"))
    q_park = ap(nc.dram_tensor("q_park", [NT * D, P], BF16, kind="Internal"))
    k_park = ap(nc.dram_tensor("k_park", [TC, D], BF16, kind="Internal"))
    v_park = ap(nc.dram_tensor("v_park", [TC, D], BF16, kind="Internal"))
    bounce = ap(nc.dram_tensor("bounce", [NT, NH * P], F32, kind="Internal"))

    shr = dict(kind="Internal") if solo else dict(kind="Internal", addr_space="Shared")
    wgt_s = ap(nc.dram_tensor("wgt_s", [D, INTER], BF16, **shr))
    wut_s = ap(nc.dram_tensor("wut_s", [D, INTER], BF16, **shr))
    wdt_s = ap(nc.dram_tensor("wdt_s", [2 * IHE, D], BF16, **shr))
    smbox = ap(nc.dram_tensor("smbox", [2 * P, D], F32, **shr))
    WDE0 = 2 * NFLG
    WDO0 = 2 * NFLG + NRBF
    FLG_S0 = 2 * NFLG + 2 * NRBF
    FLG_S1 = FLG_S0 + 1
    NFLAGS = FLG_S1 + 1
    flags_s = ap(nc.dram_tensor("flags_s", [NFLAGS, P], I32, **shr))

    with ExitStack() as top:
        sb = lambda n_, s_, d_: top.enter_context(nc.sbuf_tensor(n_, s_, d_))

        c_triu = sb("c_triu", [P, P], F32)
        c_identb = sb("c_identb", [P, P], BF16)
        c_identf = sb("c_identf", [P, P], F32)
        c_uxh = sb("c_uxh", [P, HB * P], F32)
        c_ones = sb("c_ones", [1, P], F32)
        c_mask = sb("c_mask", [1, P], F32)
        c_consts = sb("c_consts", [P, 8], F32)
        c_lutcols = sb("c_lutcols", [P, 16], F32)
        c_lutrow = sb("c_lutrow", [1, 16], F32)
        c_wf = sb("c_wf", [P, DC * NH], BF16)
        nonce_sb = sb("nonce_sb", [1, 1], I32)
        parity_sb = sb("parity_sb", [1, 1], I32)
        flagrow = sb("flagrow", [1, P], I32)
        flagrd = sb("flagrd", [1, P], I32)

        (COL_LNEPS, COL_RMSEPS, COL_ONE, COL_NEG1, COL_INVD, COL_HALF, COL_ZERO,
         COL_MASK) = range(8)
        ccol = lambda i: c_consts[:, i:i + 1]

        dma_sems = []

        def dmasem(name):
            s_ = SemCtr(nc, top, name)
            dma_sems.append(s_)
            return s_

        sx = SemCtr(nc, top, "sx")
        sg = SemCtr(nc, top, "sg")
        sld = SemCtr(nc, top, "sld")
        spoll = SemCtr(nc, top, "spoll")
        spoll2 = SemCtr(nc, top, "spoll2")
        pe = SemCtr(nc, top, "pe")
        dv = SemCtr(nc, top, "dv")
        ac = SemCtr(nc, top, "ac")
        gq = SemCtr(nc, top, "gq")
        phb = SemCtr(nc, top, "phb")

        def sv(v):
            v.wait_ge(dv.h, dv.n)

        def ss(s):
            s.wait_ge(ac.h, ac.n)

        block = top.enter_context(nc.Block())
        if getattr(build_program, "bodymap", None) is not None:
            _records = build_program.bodymap
            _raw = block

            class _BlockProxy:
                def __getattr__(self, name):
                    attr = getattr(_raw, name)
                    if name in ("sync", "gpsimd", "vector", "scalar", "tensor"):
                        def wrapped(fn, *a, **kw):
                            n0 = nc.n_instructions()
                            r = attr(fn, *a, **kw)
                            _records.append((name, fn.__name__, n0,
                                             nc.n_instructions()))
                            return r
                        return wrapped
                    return attr
            block = _BlockProxy()
        N_ENG = 5
        phase_i = [0]

        def barrier():
            phase_i[0] += 1
            tgt = N_ENG * phase_i[0]

            def fs(eng):
                eng.wait_ge(sx.h, sx.n)
                for s_ in dma_sems:
                    eng.wait_ge(s_.h, s_.n)
                eng.nop().then_inc(phb.h, 1)
                eng.wait_ge(phb.h, tgt)

            def fg(eng):
                eng.wait_ge(sg.h, sg.n)
                eng.nop().then_inc(phb.h, 1)
                eng.wait_ge(phb.h, tgt)

            def fv(eng):
                eng.wait_ge(dv.h, dv.n)
                eng.nop().then_inc(phb.h, 1)
                eng.wait_ge(phb.h, tgt)

            def fsc(eng):
                eng.wait_ge(ac.h, ac.n)
                eng.nop().then_inc(phb.h, 1)
                eng.wait_ge(phb.h, tgt)

            def fte(eng):
                eng.wait_ge(pe.h, pe.n)
                eng.nop().then_inc(phb.h, 1)
                eng.wait_ge(phb.h, tgt)
            block.sync(fs)
            block.gpsimd(fg)
            block.vector(fv)
            block.scalar(fsc)
            block.tensor(fte)
            phb.n = tgt

        def branch_dma(g, par, sem, out0, out1, in0, in1=None):
            if solo:
                sem.inc(g.dma_start(out=out0, in_=in0), 16)
                sem.inc(g.dma_start(out=out1, in_=(in0 if in1 is None else in1)), 16)
                return
            with g.If_eq(par, 0):
                g.dma_start(out=out0, in_=in0).then_inc(sem.h, 16)
            with g.Else():
                g.dma_start(out=out1, in_=(in0 if in1 is None else in1))\
                    .then_inc(sem.h, 16)
            sem.n += 16

        with nc.psum_tensor("ps_boot", [P, 512], F32) as ps_boot:
            def ph0_sync(s):
                for dst, src in [(c_triu, triu_d), (c_identb, identb_d),
                                 (c_identf, identf_d), (c_uxh, uxh_d), (c_ones, ones_d),
                                 (c_mask, mask_d), (c_consts, consts_d),
                                 (c_lutrow, lut_d), (nonce_sb, nonce_d),
                                 (parity_sb, parity_d)]:
                    sx.inc(s.dma_start(out=dst[:], in_=src), 16)
            block.sync(ph0_sync)
            sx_ph0 = sx.n

            def ph0_gp(g):
                sg.inc(g.dma_start(out=c_wf[:].rearrange("p (dc h) -> p dc h", dc=DC),
                                   in_=wf_d.rearrange("(dc p) h -> p dc h", p=P)), 16)
            block.gpsimd(ph0_gp)

            def ph0_vec(v):
                v.wait_ge(sx.h, sx_ph0)
                sv(v); dv.inc(v.tensor_scalar(flagrow[:], nonce_sb[:].broadcast_to((1, P)),
                                              0, None, op0=ALU.add))
            block.vector(ph0_vec)

            def ph0_te(te):
                te.wait_ge(sx.h, sx_ph0)
                pe.inc(te.matmul(ps_boot[:, 0:16], c_ones[:], c_lutrow[:],
                                 start=True, stop=True))
            block.tensor(ph0_te)
            pe_ph0 = pe.n

            def ph0_vec2(v):
                v.wait_ge(pe.h, pe_ph0)
                sv(v); dv.inc(v.tensor_copy(c_lutcols[:], ps_boot[:, 0:16]))
            block.vector(ph0_vec2)
        barrier()
        if phases < 1:
            return nc

        DW = min(512, D)
        OCINS = c.OSG * max(1, (OCH // 2) // c.OSG)
        NTRH = DC * NTT
        STW = 2 if solo else 1

        fstk = ExitStack()
        fsb = lambda n_, s_, d_: fstk.enter_context(nc.sbuf_tensor(n_, s_, d_))
        mat_bf = [fsb(f"mat_bf{i}", [P, DC * P], BF16) for i in range(2)]
        mat_pl = fsb("mat_pl", [P, DC * P], BF16)
        mat_ac = [fsb(f"mat_ac{i}", [P, DC * P], BF16) for i in range(2)]

        dv1 = dv.n
        slda = [dmasem("slda0"), dmasem("slda1")]
        sldb = [dmasem("sldb0"), dmasem("sldb1")]
        sgsa = [dmasem("sgsa0"), dmasem("sgsa1")]
        sgsb = [dmasem("sgsb0"), dmasem("sgsb1")]

        ffn_slots = ([dmasem("slx0"), dmasem("slx1")],
                     [dmasem("slg0"), dmasem("slg1")],
                     [dmasem("sld0"), dmasem("sld1")],
                     dmasem("slr"), dmasem("slo"),
                     [dmasem("slxg0"), dmasem("slxg1")])

        SILU_MARGIN = 3
        script = []
        for grp in range(NFLG):
            for oc in range(grp * c.OSG, min((grp + 1) * c.OSG, OCH)):
                script.append(("mata", oc))
            if grp >= SILU_MARGIN and phases >= 2:
                g2 = grp - SILU_MARGIN
                for oc in range(g2 * c.OSG, min((g2 + 1) * c.OSG, OCH)):
                    script.append(("silu", oc))
        if phases >= 2:
            for grp in range(max(0, NFLG - SILU_MARGIN), NFLG):
                for oc in range(grp * c.OSG, min((grp + 1) * c.OSG, OCH)):
                    script.append(("silu", oc))
        for rb in range(NRBF):
            script.append(("matb", rb))
        dv_mata, dv_matb, dv_silu_pred = {}, {}, {}

        def walk_script(base):
            pos = base
            for kind, i in script:
                if kind == "silu":
                    pos += 1
                    dv_silu_pred[i] = pos
                else:
                    pos += 32
                    (dv_mata if kind == "mata" else dv_matb)[i] = pos
            return pos

        gq_base = gq.n

        def gq_gu(grp): return gq_base + grp + 1
        def gq_wdp(i): return gq_base + NFLG + i + 1
        gq_final = gq_base + NFLG + NRBF

        def make_ffn(tag, src, dst, first_pass, stack, ps_fn):
            F = {}
            b = lambda n_, s_, d_: stack.enter_context(nc.sbuf_tensor(n_, s_, d_))
            xb = [b(f"{tag}_x{i}", [P, D], F32) for i in range(2)]
            xsq = b(f"{tag}_xsq", [P, D], F32)
            stat = b(f"{tag}_stat", [P, 8], F32)
            xlt = [b(f"{tag}_xlt{i}", [P, DC * TF], BF16) for i in range(2)]
            xln_tok = b(f"{tag}_xlntok", [P, D], F32)
            h_sb = b(f"{tag}_h", [P, OCH * TF], BF16)
            NW = c.OSG * P
            wgp = [b(f"{tag}_wgp{i}", [P, DC * NW], BF16) for i in range(2)]
            wup = [b(f"{tag}_wup{i}", [P, DC * NW], BF16) for i in range(2)]
            sgl = b(f"{tag}_sgl", [P, TF], F32)
            wds = [b(f"{tag}_wds{i}", [P, D], BF16) for i in range(2)]
            xr = b(f"{tag}_xr", [P, D], F32)
            ot = b(f"{tag}_ot", [P, D], F32)

            (xslot, guslot, dnslot, xrslot, otslot, xgslot) = ffn_slots
            ps_g = ps_fn[0][:, 0:TF]
            ps_u = ps_fn[1][:, 0:TF]

            st = dict(free={},
                      xs={}, sq={}, srt={}, xln={},
                      tr={}, tr_done={}, evac_done={}, stats_done={},
                      sil_done={}, dvec_done={}, last_tr_pe=0,
                      gu_base={})
            F["st"] = st

            LNV = 7 + DC

            def tr_slot(dc_):
                return ps_fn[2 + (dc_ % 2)][:, 0:P]

            def ln_load(half, tts, fence=(), pred_dv=None, pred_ac=None, gp=False):
                slots_ = xgslot if gp else xslot

                def body(s):
                    for h_, v_ in fence:
                        s.wait_ge(h_, v_)
                    for tt in tts:
                        slot = (half * NTT + tt) % 2
                        gprev = half * NTT + tt - 2
                        if gprev >= 0:
                            key = (gprev // NTT, gprev % NTT)
                            if key in st["free"]:
                                dvf, acf = st["free"][key]
                                s.wait_ge(dv.h, dvf)
                                s.wait_ge(ac.h, acf)
                            else:
                                ptt = key[1]
                                s.wait_ge(dv.h, pred_dv + LNV * ptt + 7)
                                if key in st["sq"]:
                                    s.wait_ge(ac.h, st["sq"][key])
                                else:
                                    s.wait_ge(ac.h, pred_ac + 2 * ptt + 1)
                        slots_[slot].inc(s.dma_start(
                            out=xb[slot][:],
                            in_=src[half * TF + tt * P:half * TF + (tt + 1) * P, :]),
                            16)
                        rec = (slots_[slot].h, slots_[slot].n)
                        if (half, tt) in st["xs"]:
                            assert st["xs"][(half, tt)] == rec, (tag, half, tt)
                        st["xs"][(half, tt)] = rec
                if gp:
                    block.gpsimd(body)
                else:
                    block.sync(body)
            F["ln_load"] = ln_load

            def ln_promise(half):
                cnt = {0: 0, 1: 0}
                for tt in range(NTT):
                    if (half, tt) not in st["xs"]:
                        slot = (half * NTT + tt) % 2
                        sl = xgslot if tt >= 2 else xslot
                        cnt[slot] += 1
                        st["xs"][(half, tt)] = (sl[slot].h,
                                                sl[slot].n + 16 * cnt[slot])

            def ln_sq_op(s, half, tt, dv0p):
                s.wait_ge(*st["xs"][(half, tt)])
                if tt >= 1:
                    s.wait_ge(dv.h, dv0p + LNV * (tt - 1) + 3)
                elif half >= 1:
                    s.wait_ge(dv.h, st["stats_done"][half - 1])
                ss(s); ac.inc(s.activation(xsq[:], xb[(half * NTT + tt) % 2][:],
                                           AF.Square, accum_out=stat[:, 2:3]))
                st["sq"][(half, tt)] = ac.n

            def ln_srt_op(s, half, tt):
                dv0p = st[("dv0p", half)]
                s.wait_ge(dv.h, dv0p + LNV * tt + 5)
                if tt >= 1:
                    s.wait_ge(dv.h, dv0p + LNV * (tt - 1) + 6)
                ss(s); ac.inc(s.activation(stat[:, 4:5], stat[:, 3:4], AF.Sqrt,
                                           bias=ccol(COL_LNEPS)))
                st["srt"][(half, tt)] = ac.n

            def ln_acinline_ops(s, half):
                dv0p = st[("dv0p", half)]
                for tt in range(1, NTT):
                    ln_sq_op(s, half, tt, dv0p)
                    ln_srt_op(s, half, tt - 1)
                ln_srt_op(s, half, NTT - 1)

            def ln_sc(half, base=None):
                dv0p = dv.n if base is None else base
                st[("dv0p", half)] = dv0p
                ln_promise(half)

                def body(s):
                    for tt in range(NTT if base is None else 1):
                        ln_sq_op(s, half, tt, dv0p)
                        if base is None:
                            s.wait_ge(dv.h, dv0p + LNV * tt + 5)
                            if tt >= 1:
                                s.wait_ge(dv.h, dv0p + LNV * (tt - 1) + 6)
                            ss(s); ac.inc(s.activation(stat[:, 4:5], stat[:, 3:4],
                                                       AF.Sqrt,
                                                       bias=ccol(COL_LNEPS)))
                            st["srt"][(half, tt)] = ac.n
                block.scalar(body)
            F["ln_sc"] = ln_sc

            def ln_stats_ops(v, half, tr_pred):
                dv0p = st[("dv0p", half)]
                assert dv.n == dv0p, (tag, half, dv.n, dv0p)
                for tt in range(NTT):
                    slot = (half * NTT + tt) % 2
                    v.wait_ge(*st["xs"][(half, tt)])
                    sv(v); dv.inc(v.reduce_sum(stat[:, 0:1], xb[slot][:], axis=AX.X))
                    sv(v); dv.inc(v.tensor_scalar(stat[:, 1:2], stat[:, 0:1], 1.0 / D,
                                                  None, op0=ALU.mult))
                    v.wait_ge(ac.h, st["sq"][(half, tt)])
                    if tt >= 1:
                        v.wait_ge(ac.h, st["srt"][(half, tt - 1)])
                    sv(v); dv.inc(v.tensor_scalar(stat[:, 3:4], stat[:, 2:3], 1.0 / D,
                                                  None, op0=ALU.mult))
                    sv(v); dv.inc(v.tensor_tensor(stat[:, 5:6], stat[:, 1:2],
                                                  stat[:, 1:2], op=ALU.mult))
                    sv(v); dv.inc(v.tensor_tensor(stat[:, 3:4], stat[:, 3:4],
                                                  stat[:, 5:6], op=ALU.subtract))
                    v.wait_ge(ac.h, st["srt"][(half, tt)])
                    sv(v); dv.inc(v.reciprocal(stat[:, 6:7], stat[:, 4:5]))
                    if tt >= 1:
                        rec = st["tr"].get((half, tt - 1, DC - 1))
                        if rec is not None:
                            assert rec == tr_pred(tt - 1, DC - 1), (tag, half, tt)
                        v.wait_ge(pe.h, tr_pred(tt - 1, DC - 1))
                    elif st["last_tr_pe"]:
                        v.wait_ge(pe.h, st["last_tr_pe"])
                    sv(v); dv.inc(v.tensor_scalar(xln_tok[:], xb[slot][:], stat[:, 1:2],
                                                  stat[:, 6:7], op0=ALU.subtract,
                                                  op1=ALU.mult))
                    st["xln"][(half, tt)] = dv.n
                    st["free"][(half, tt)] = (dv.n, st["sq"][(half, tt)])
                    for dc_ in range(DC):
                        rec = st["tr"].get((half, tt, dc_))
                        if rec is not None:
                            assert rec == tr_pred(tt, dc_), (tag, half, tt, dc_)
                        v.wait_ge(pe.h, tr_pred(tt, dc_))
                        sv(v); dv.inc(v.tensor_copy(
                            xlt[half % 2][:, dc_ * TF + tt * P:dc_ * TF + (tt + 1) * P],
                            tr_slot(dc_)))
                        assert dv.n == dv0p + LNV * tt + 8 + dc_, (tag, half, tt, dc_)
                st["stats_done"][half] = dv.n
                st["evac_done"][half] = dv.n
                st["last_tr_pe"] = tr_pred(NTT - 1, DC - 1)

            def ln_stats(half, tr_pred):
                def body(v):
                    ln_stats_ops(v, half, tr_pred)
                block.vector(body)
            F["ln_stats"] = ln_stats

            def ln_tr_ops(te, half):
                dv0p = st[("dv0p", half)]
                for tt in range(NTT):
                    te.wait_ge(dv.h, dv0p + LNV * tt + 7)
                    for dc_ in range(DC):
                        if dc_ >= 2:
                            te.wait_ge(dv.h, dv0p + LNV * tt + 8 + (dc_ - 2))
                        elif tt >= 1:
                            te.wait_ge(dv.h, dv0p + LNV * (tt - 1) + 8 + DC - 2 + dc_)
                        pe.inc(te.transpose(tr_slot(dc_),
                                            xln_tok[:, dc_ * P:(dc_ + 1) * P],
                                            c_identf[:]))
                        st["tr"][(half, tt, dc_)] = pe.n
                st["tr_done"][half] = pe.n

            def ln_te(half, pre_dv=None):
                def body(te):
                    if pre_dv:
                        te.wait_ge(dv.h, pre_dv)
                    ln_tr_ops(te, half)
                block.tensor(body)
            F["ln_te"] = ln_te

            def trofs(half, oc):
                return NTRH if (st.get(("gu_tr", half)) and oc >= OCINS) else 0

            def pe_g(half, oc): return st[("peg0", half)] + 2 * oc + 1 + trofs(half, oc)
            def pe_u(half, oc): return st[("peg0", half)] + 2 * oc + 2 + trofs(half, oc)
            def ac_sil(half, oc):
                return st[("acg0", half)] + oc + 1 + \
                    (2 * NTT - 1 if (st.get(("gu_sqrt", half)) and oc >= OCINS)
                     else 0)

            NST = (7 + DC) * NTT

            def dv_h(half, oc):
                return st[("dvh0", half)] + oc + 1 + \
                    (NST if (st.get(("gu_tr", half)) and oc >= OCINS) else 0)
            F["pe_u"] = pe_u
            F["ac_sil"] = ac_sil

            def gu_done(half, grp):
                g0 = st["gu_base"][half]
                return (guslot[grp % 2].h, g0[grp % 2] + 32 * (grp // 2 + 1))

            def gu_sync(half, chase=False, has_tr=False):
                st["gu_base"][half] = [guslot[0].n, guslot[1].n]
                st[("peg0", half)] = pe.n
                st[("dvh0", half)] = dv.n
                st[("gu_tr", half)] = has_tr

                def body(s):
                    for grp in range(NFLG):
                        if chase:
                            s.wait_ge(gq.h, gq_gu(grp))
                        elif first_pass and grp == 0:
                            s.wait_ge(gq.h, gq_base + NFLG)
                        o0 = grp * NW
                        o1 = min(o0 + NW, INTER)
                        if grp >= 2:
                            lastoc = min((grp - 1) * c.OSG, OCH) - 1
                            s.wait_ge(pe.h, pe_u(half, lastoc))
                        elif grp == 0:
                            s.wait_ge(pe.h, st[("peg0", half)])
                        guslot[grp % 2].inc(s.dma_start(
                            out=wgp[grp % 2][:, 0:DC * (o1 - o0)]
                                .rearrange("p (dc o) -> p dc o", dc=DC),
                            in_=wgt_s[:, o0:o1]
                                .rearrange("(dc p) o -> p dc o", p=P)), 16)
                        guslot[grp % 2].inc(s.dma_start(
                            out=wup[grp % 2][:, 0:DC * (o1 - o0)]
                                .rearrange("p (dc o) -> p dc o", dc=DC),
                            in_=wut_s[:, o0:o1]
                                .rearrange("(dc p) o -> p dc o", p=P)), 16)
                block.sync(body)
            F["gu_sync"] = gu_sync

            def gu_main(half, nxt=None, chase=False, dvh_fn=None):
                assert st[("gu_tr", half)] == (nxt is not None)
                st[("acg0", half)] = ac.n
                st[("gu_sqrt", half)] = nxt is not None
                if dvh_fn is None:
                    dvh_fn = lambda oc: dv_h(half, oc)
                assert pe.n == st[("peg0", half)], (tag, half)
                if not chase:
                    assert dv.n == st[("dvh0", half)], (tag, half, dv.n)
                prev_done = st["dvec_done"].get(half - 1)

                def body(te):
                    if prev_done:
                        te.wait_ge(dv.h, prev_done)
                    te.wait_ge(dv.h, st["evac_done"][half])
                    inserted = [False]
                    for grp in range(NFLG):
                        o0 = grp * NW
                        ow = min((grp + 1) * NW, INTER) - o0
                        if chase:
                            te.wait_ge(gq.h, gq_gu(grp))
                        te.wait_ge(*gu_done(half, grp))
                        for ocw in range(ow // P):
                            oc = grp * c.OSG + ocw
                            if nxt is not None and not inserted[0] and oc >= OCINS:
                                assert pe.n == st[("peg0", half)] + 2 * OCINS
                                ln_tr_ops(te, nxt)
                                inserted[0] = True
                            if oc >= 1:
                                te.wait_ge(ac.h, ac_sil(half, oc - 1))
                            for n2 in range(GW):
                                for dc_ in range(DC):
                                    mm = te.matmul(
                                        ps_g[:, n2 * GWW:(n2 + 1) * GWW],
                                        wgp[grp % 2][:, dc_ * ow + ocw * P:
                                                     dc_ * ow + (ocw + 1) * P],
                                        xlt[half % 2][:, dc_ * TF + n2 * GWW:
                                                      dc_ * TF + (n2 + 1) * GWW],
                                        start=(dc_ == 0), stop=(dc_ == DC - 1))
                            pe.inc(mm)
                            assert pe.n == pe_g(half, oc)
                            if oc >= 1:
                                te.wait_ge(dv.h, dvh_fn(oc - 1))
                            for n2 in range(GW):
                                for dc_ in range(DC):
                                    mm = te.matmul(
                                        ps_u[:, n2 * GWW:(n2 + 1) * GWW],
                                        wup[grp % 2][:, dc_ * ow + ocw * P:
                                                     dc_ * ow + (ocw + 1) * P],
                                        xlt[half % 2][:, dc_ * TF + n2 * GWW:
                                                      dc_ * TF + (n2 + 1) * GWW],
                                        start=(dc_ == 0), stop=(dc_ == DC - 1))
                            pe.inc(mm)
                    if nxt is not None and not inserted[0]:
                        ln_tr_ops(te, nxt)
                block.tensor(body)
            F["gu_main"] = gu_main

            def gu_sc(half, dvh_fn=None, sqrt_half=None):
                assert st[("gu_sqrt", half)] == (sqrt_half is not None), (tag, half)
                if dvh_fn is None:
                    dvh_fn = lambda oc: dv_h(half, oc)
                assert ac.n == st[("acg0", half)], (tag, half)

                def body(s):
                    for oc in range(OCH):
                        if sqrt_half is not None and oc == OCINS:
                            ln_acinline_ops(s, sqrt_half)
                        s.wait_ge(pe.h, pe_g(half, oc))
                        if oc >= 1:
                            s.wait_ge(dv.h, dvh_fn(oc - 1))
                        ss(s); ac.inc(s.activation(sgl[:], ps_g[:], AF.Silu))
                        assert ac.n == ac_sil(half, oc)
                block.scalar(body)
            F["gu_sc"] = gu_sc

            def silu_op(v, half, oc):
                v.wait_ge(ac.h, ac_sil(half, oc))
                v.wait_ge(pe.h, pe_u(half, oc))
                sv(v); dv.inc(v.tensor_tensor(h_sb[:, oc * TF:(oc + 1) * TF],
                                              sgl[:], ps_u[:], op=ALU.mult))
            F["silu_op"] = silu_op

            def gu_vec(half, nxt=None):
                assert dv.n == st[("dvh0", half)], (tag, half, dv.n)

                def body(v):
                    for oc in range(OCINS if nxt is not None else OCH):
                        silu_op(v, half, oc)
                        assert dv.n == dv_h(half, oc)
                    if nxt is not None:
                        peg0 = st[("peg0", half)]
                        ln_stats_ops(v, nxt, lambda tt, dc_:
                                     peg0 + 2 * OCINS + tt * DC + dc_ + 1)
                        for oc in range(OCINS, OCH):
                            silu_op(v, half, oc)
                            assert dv.n == dv_h(half, oc)
                block.vector(body)
                st["sil_done"][half] = dv.n
            F["gu_vec"] = gu_vec

            def ocseq_of(chase):
                if not chase:
                    return list(range(OCH))
                seq = []
                for i in range(NRBF):
                    if i < OCH:
                        seq.append(i)
                    if NRBF + i < OCH:
                        seq.append(NRBF + i)
                return seq

            def dn_sync(half, chase=False):
                dns0 = [dnslot[0].n, dnslot[1].n]
                xrs0 = xrslot.n
                ots0 = otslot.n
                ped0 = pe.n
                dvd0 = dv.n
                st[("dn", half)] = (dns0, xrs0, ots0, ped0, dvd0)
                ocseq = ocseq_of(chase)
                NE = NTG * NGD

                prev_evac = st["dvec_done"].get(half - 1, 0)

                def body(s):
                    if first_pass and not chase:
                        s.wait_ge(gq.h, gq_final)

                    def xr_load(e):
                        if e == 0:
                            if prev_evac:
                                s.wait_ge(dv.h, prev_evac)
                        else:
                            s.wait_ge(dv.h, dvd0 + e * NDS)
                        tg_e, j_e = e // NGD, e % NGD
                        xrslot.inc(s.dma_start(
                            out=xr[:],
                            in_=src[half * TF + tg_e * TG + j_e * P:
                                    half * TF + tg_e * TG + (j_e + 1) * P, :]), 16)

                    xr_load(0)
                    nxt_e = 1
                    for tg_ in range(NTG):
                        for seq, oc in enumerate(ocseq):
                            k = tg_ * OCH + seq
                            if chase:
                                s.wait_ge(gq.h,
                                          gq_wdp(oc if oc < NRBF else oc - NRBF))
                            s.wait_ge(pe.h, ped0 if k < 2 else ped0 + k - 1)
                            dnslot[k % 2].inc(s.dma_start(
                                out=wds[k % 2][:],
                                in_=wdt_s[oc * P:(oc + 1) * P, :]), 16)
                        while nxt_e <= min((tg_ + 1) * NGD, NE - 1):
                            xr_load(nxt_e)
                            nxt_e += 1
                block.sync(body)
            F["dn_sync"] = dn_sync

            def dn_te(half, chase=False, dvh_last=None, evac_next=None):
                (dns0, xrs0, ots0, ped0, dvd0) = st[("dn", half)]
                assert pe.n == ped0, (tag, half)
                if dvh_last is None:
                    dvh_last = st["sil_done"][half]
                ocseq = ocseq_of(chase)

                def body(te):
                    te.wait_ge(dv.h, dvh_last)
                    if evac_next:
                        te.wait_ge(dv.h, evac_next)
                    for tg_ in range(NTG):
                        if tg_ >= 1:
                            te.wait_ge(dv.h, dvd0 + tg_ * NGD * NDS)
                        for seq, oc in enumerate(ocseq):
                            k = tg_ * OCH + seq
                            if chase:
                                te.wait_ge(gq.h,
                                           gq_wdp(oc if oc < NRBF else oc - NRBF))
                            te.wait_ge(dnslot[k % 2].h,
                                       dns0[k % 2] + 16 * (k // 2 + 1))
                            for j in range(NGD):
                                for ds_ in range(NDS):
                                    mm = te.matmul(
                                        ps_fn[j * NDS + ds_][:, 0:DW],
                                        h_sb[:, oc * TF + tg_ * TG + j * P:
                                             oc * TF + tg_ * TG + (j + 1) * P],
                                        wds[k % 2][:, ds_ * DW:(ds_ + 1) * DW],
                                        start=(seq == 0), stop=(seq == OCH - 1))
                            pe.inc(mm)
                block.tensor(body)
            F["dn_te"] = dn_te

            def dn_vec(half):
                (dns0, xrs0, ots0, ped0, dvd0) = st[("dn", half)]
                assert dv.n == dvd0, (tag, half, dv.n)

                def body(v):
                    for tg_ in range(NTG):
                        v.wait_ge(pe.h, ped0 + (tg_ + 1) * OCH)
                        for j in range(NGD):
                            e = tg_ * NGD + j
                            v.wait_ge(xrslot.h, xrs0 + 16 * (e + 1))
                            v.wait_ge(otslot.h, ots0 + 16 * e)
                            for ds_ in range(NDS):
                                sv(v); dv.inc(v.scalar_tensor_tensor(
                                    ot[:, ds_ * DW:(ds_ + 1) * DW],
                                    ps_fn[j * NDS + ds_][:, 0:DW], 0.5,
                                    xr[:, ds_ * DW:(ds_ + 1) * DW],
                                    op0=ALU.mult, op1=ALU.add))
                block.vector(body)
                st["dvec_done"][half] = dv.n
                assert dv.n == dvd0 + NTG * NGD * NDS
            F["dn_vec"] = dn_vec

            def dn_store(half):
                (dns0, xrs0, ots0, ped0, dvd0) = st[("dn", half)]

                def body(g):
                    for tg_ in range(NTG):
                        for j in range(NGD):
                            e = tg_ * NGD + j
                            g.wait_ge(dv.h, dvd0 + (e + 1) * NDS)
                            otslot.inc(g.dma_start(
                                out=dst[half * TF + tg_ * TG + j * P:
                                        half * TF + tg_ * TG + (j + 1) * P, :],
                                in_=ot[:]), 16)
                    g.wait_ge(otslot.h, otslot.n)
                block.gpsimd(body)
            F["dn_store"] = dn_store

            def template(half):
                import os as _os
                nxt = half + 1 if half + 1 < c.NHALF else None
                if _os.environ.get("TPLNOINL"):
                    nxt = None
                gu_sync(half, has_tr=(nxt is not None))
                if nxt is not None:
                    ln_sc(nxt, base=st[("dvh0", half)] + OCINS)
                gu_main(half, nxt=nxt)
                gu_sc(half, sqrt_half=nxt)
                if nxt is not None and NTT > 2:
                    ln_load(nxt, range(2, NTT), pred_dv=st[("dv0p", nxt)], gp=True)
                if nxt is not None and half + 2 < c.NHALF:
                    ln_load(half + 2, range(min(2, NTT)),
                            pred_dv=st[("dv0p", half + 1)])
                gu_vec(half, nxt=nxt)
                dn_sync(half)
                dn_te(half, evac_next=(st["evac_done"].get(nxt)
                                       if nxt is not None else None))
                dn_vec(half)
                dn_store(half)
            F["template"] = template

            def prologue(fence=()):
                ln_load(0, range(NTT), fence=fence, pred_dv=dv.n, pred_ac=ac.n)
                ln_sc(0)
                pebase = pe.n
                ln_stats(0, lambda tt, dc_, pb=pebase: pb + tt * DC + dc_ + 1)
                ln_te(0)
                if c.NHALF > 1:
                    ln_load(1, range(min(2, NTT)))
            F["prologue"] = prologue
            return F

        ps_f1 = [fstk.enter_context(nc.psum_tensor(f"f1_ps{i}", [P, 512], F32))
                 for i in range(max(4, NGD * NDS))]
        ffn1 = make_ffn("f1", x_in, x1_park, True, fstk, ps_f1)

        ffn1["prologue"](fence=[(phb.h, N_ENG * phase_i[0])])
        dv_mat_end = walk_script(dv.n)

        def slda_done(oc): return (slda[oc % 2].h, 16 * (oc // 2 + 1))
        def sldb_done(rb): return (sldb[rb % 2].h, 16 * (rb // 2 + 1))

        def ph1_gp(g):
            def load_a(oc):
                if oc >= 2:
                    g.wait_ge(dv.h, dv_mata[oc - 2])
                slda[oc % 2].inc(g.dma_start(
                    out=mat_bf[oc % 2][:].rearrange("p (a o) -> p a o", a=DC),
                    in_=mata_d[oc * D:(oc + 1) * D, :]
                        .rearrange("(a p) o -> p a o", p=P)), 16)

            def load_b(rb):
                if rb >= 2:
                    g.wait_ge(dv.h, dv_matb[rb - 2])
                else:
                    g.wait_ge(dv.h, dv_mata[OCH - 1])
                sldb[rb % 2].inc(g.dma_start(out=mat_bf[rb % 2][:],
                                             in_=matb_d[rb * P:(rb + 1) * P, :]), 16)

            with g.register("par") as par, g.register("want") as want, \
                 g.register("seen") as seen, g.register("more") as more, \
                 g.register("tgt") as tgt:
                g.load(par, parity_sb[0:1, 0:1])
                g.load(want, nonce_sb[0:1, 0:1])
                g.reg_mov(tgt, 0)

                def poll_row(row):
                    g.reg_mov(more, 1)
                    with g.While(more):
                        g.reg_add(tgt, tgt, 16)
                        g.dma_start(out=flagrd[:],
                                    in_=flags_s[row:row + 1, :]).then_inc(spoll.h, 16)
                        g.wait_ge(spoll.h, tgt)
                        g.load(seen, flagrd[0:1, 0:1])
                        g.reg_alu(more, seen, want, ALU.not_equal)

                load_a(0)
                if OCH > 1:
                    load_a(1)
                for grp in range(NFLG):
                    for oc in range(grp * c.OSG, min((grp + 1) * c.OSG, OCH)):
                        if oc + 2 < OCH:
                            load_a(oc + 2)
                        g.wait_ge(dv.h, dv_mata[oc])
                        branch_dma(g, par, sgsa[oc % 2],
                                   wgt_s[:, oc * P:(oc + 1) * P]
                                   .rearrange("(a p) o -> p a o", p=P),
                                   wut_s[:, oc * P:(oc + 1) * P]
                                   .rearrange("(a p) o -> p a o", p=P),
                                   mat_ac[oc % 2][:].rearrange("p (a o) -> p a o",
                                                               a=DC))
                    g.wait_ge(sgsa[0].h, sgsa[0].n)
                    g.wait_ge(sgsa[1].h, sgsa[1].n)
                    branch_dma(g, par, sg, flags_s[grp:grp + 1, :],
                               flags_s[NFLG + grp:NFLG + grp + 1, :], flagrow[:])
                    g.wait_ge(sg.h, sg.n)
                    if grp >= 1:
                        poll_row(grp - 1)
                        poll_row(NFLG + grp - 1)
                        gq.inc(g.nop())
                poll_row(NFLG - 1)
                poll_row(2 * NFLG - 1)
                gq.inc(g.nop())
                load_b(0)
                if NRBF > 1:
                    load_b(1)
                for rb in range(NRBF):
                    if rb + 2 < NRBF:
                        load_b(rb + 2)
                    g.wait_ge(dv.h, dv_matb[rb])
                    branch_dma(g, par, sgsb[rb % 2],
                               wdt_s[rb * P:(rb + 1) * P, :],
                               wdt_s[IHE + rb * P:IHE + (rb + 1) * P, :],
                               mat_ac[rb % 2][:])
                    g.wait_ge(sgsb[rb % 2].h, sgsb[rb % 2].n)
                    branch_dma(g, par, sg, flags_s[WDE0 + rb:WDE0 + rb + 1, :],
                               flags_s[WDO0 + rb:WDO0 + rb + 1, :], flagrow[:])
                    g.wait_ge(sg.h, sg.n)
                    if rb >= 1:
                        poll_row(WDE0 + rb - 1)
                        poll_row(WDO0 + rb - 1)
                        gq.inc(g.nop())
                poll_row(WDE0 + NRBF - 1)
                poll_row(WDO0 + NRBF - 1)
                gq.inc(g.nop())
        block.gpsimd(ph1_gp)
        assert gq.n == gq_final, (gq.n, gq_final)

        ffn1["st"][("peg0", 0)] = pe.n
        ffn1["st"][("acg0", 0)] = ac.n
        ffn1["st"][("gu_tr", 0)] = False
        dv_silu = {}

        def ph1_vec(v):
            NEV = (OCH + 1) // 2
            NOD = OCH // 2
            for kind, idx in script:
                if kind == "silu":
                    ffn1["silu_op"](v, 0, idx)
                    dv_silu[idx] = dv.n
                    assert dv.n == dv_silu_pred[idx], (idx, dv.n)
                    continue
                if kind == "mata":
                    oc = idx
                    v.wait_ge(*slda_done(oc))
                    v.wait_ge(sgsa[oc % 2].h, 16 * STW * (oc // 2))
                    buf, tgtpos = mat_bf[oc % 2][:], dv_mata[oc]
                    acc = mat_ac[oc % 2][:]
                else:
                    rb = idx
                    v.wait_ge(*sldb_done(rb))
                    if rb < 2:
                        v.wait_ge(sgsa[rb % 2].h, 16 * STW * (NEV if rb == 0 else NOD))
                    else:
                        v.wait_ge(sgsb[rb % 2].h, 16 * STW * (rb // 2))
                    buf, tgtpos = mat_bf[rb % 2][:], dv_matb[rb]
                    acc = mat_ac[rb % 2][:]
                for t in range(16):
                    sv(v); dv.inc(v.tensor_scalar(mat_pl[:], buf, float(t),
                                                  c_lutcols[:, t:t + 1],
                                                  op0=ALU.is_equal, op1=ALU.mult))
                    sv(v)
                    if t == 0:
                        dv.inc(v.tensor_copy(acc, mat_pl[:]))
                    else:
                        dv.inc(v.tensor_tensor(acc, acc, mat_pl[:], op=ALU.add))
                assert dv.n == tgtpos, (kind, idx, dv.n, tgtpos)
        block.vector(ph1_vec)
        assert dv.n == dv_mat_end

        if phases < 2:
            fstk.close()
            barrier()
            return nc

        ffn1["gu_sync"](0, chase=True, has_tr=False)
        ffn1["gu_main"](0, nxt=None, chase=True, dvh_fn=lambda oc: dv_silu[oc])
        ffn1["gu_sc"](0, dvh_fn=lambda oc: dv_silu[oc])
        ffn1["st"]["sil_done"][0] = dv_silu[OCH - 1]
        if ffn_parts < 2:
            fstk.close()
            barrier()
            return nc

        ffn1["dn_sync"](0, chase=True)
        ffn1["dn_te"](0, chase=True, dvh_last=dv_silu[OCH - 1])
        ffn1["dn_vec"](0)
        ffn1["dn_store"](0)
        if c.NHALF > 1:
            ffn1["ln_sc"](1)
            pebase = pe.n
            ffn1["ln_stats"](1, lambda tt, dc_, pb=pebase: pb + tt * DC + dc_ + 1)
            if NTT > 2:
                ffn1["ln_load"](1, range(2, NTT), gp=True)
            if c.NHALF > 2:
                ffn1["ln_load"](2, range(min(2, NTT)))
            ffn1["ln_te"](1, pre_dv=ffn1["st"]["dvec_done"][0])

        if ffn_parts >= 3:
            hi = c.NHALF if ffn_parts >= 5 else min(c.NHALF, ffn_parts - 1)
            for half in range(1, hi):
                ffn1["template"](half)
        fstk.close()
        barrier()
        if phases < 3:
            return nc

        attn_stack = ExitStack()
        atsb = lambda n_, s_, d_: attn_stack.enter_context(nc.sbuf_tensor(n_, s_, d_))
        S_sb = atsb("S_sb", [P, D], F32)
        S_bf = atsb("S_bf", [P, D], BF16)
        SR_bf = atsb("SR_bf", [P, D], BF16)
        ld_all = atsb("ld_all", [P, NT * NH], F32)
        expcum_all = atsb("expcum_all", [P, NT * NH], F32)
        Rtab = atsb("Rtab", [1, (NT + 1) * NH], F32)

        def emit_ln(tag, n_tiles, src_rows, xlt, xstride, xb, xsq, stat, xln_tok,
                    ps_tr, xslot, pre_pe_wait=None, fence=()):
            sx0, ac0, dv0, pe0 = sx.n, ac.n, dv.n, pe.n
            xs0 = [xslot[0].n, xslot[1].n]

            def xs_done(tt):
                return (xslot[tt % 2].h, xs0[tt % 2] + 16 * (tt // 2 + 1))
            LNV = 7 + DC

            def dv_tile(tt): return dv0 + LNV * (tt + 1)
            def dv_xln(tt): return dv0 + LNV * tt + 7
            def dv_var(tt): return dv0 + LNV * tt + 5
            def dv_evac(tt, dc_): return dv0 + LNV * tt + 8 + dc_
            def ac_sq(tt): return ac0 + 2 * tt + 1
            def ac_srt(tt): return ac0 + 2 * tt + 2
            def pe_tr(tt, dc_): return pe0 + DC * tt + dc_ + 1

            def f_sync(s):
                for h_, v_ in fence:
                    s.wait_ge(h_, v_)
                for tt in range(n_tiles):
                    if tt >= 2:
                        s.wait_ge(dv.h, dv_tile(tt - 2))
                        s.wait_ge(ac.h, ac_srt(tt - 2))
                    else:
                        s.wait_ge(dv.h, dv0)
                        s.wait_ge(ac.h, ac0)
                    xslot[tt % 2].inc(s.dma_start(out=xb[tt % 2][:], in_=src_rows(tt)), 16)
            block.sync(f_sync)

            def f_sc(s):
                for h_, v_ in fence:
                    s.wait_ge(h_, v_)
                for tt in range(n_tiles):
                    s.wait_ge(*xs_done(tt))
                    if tt >= 1:
                        s.wait_ge(dv.h, dv_var(tt - 1))
                    ss(s); ac.inc(s.activation(xsq[tt % 2][:], xb[tt % 2][:], AF.Square,
                                               accum_out=stat[:, 2:3]))
                    s.wait_ge(dv.h, dv_var(tt))
                    ss(s); ac.inc(s.activation(stat[:, 4:5], stat[:, 3:4], AF.Sqrt,
                                               bias=ccol(COL_LNEPS)))
            block.scalar(f_sc)

            def f_te(te):
                if pre_pe_wait is not None:
                    te.wait_ge(pre_pe_wait[0], pre_pe_wait[1])
                for tt in range(n_tiles):
                    te.wait_ge(dv.h, dv_xln(tt))
                    for dc_ in range(DC):
                        if dc_ >= 2:
                            te.wait_ge(dv.h, dv_evac(tt, dc_ - 2))
                        sl_ = (dc_ % 2) * 1024
                        pe.inc(te.transpose(ps_tr[:, sl_:sl_ + P],
                                            xln_tok[:, dc_ * P:(dc_ + 1) * P],
                                            c_identb[:]))
            block.tensor(f_te)

            def f_vec(v):
                for h_, v_ in fence:
                    v.wait_ge(h_, v_)
                for tt in range(n_tiles):
                    v.wait_ge(*xs_done(tt))
                    sv(v); dv.inc(v.reduce_sum(stat[:, 0:1], xb[tt % 2][:], axis=AX.X))
                    sv(v); dv.inc(v.tensor_scalar(stat[:, 1:2], stat[:, 0:1], 1.0 / D,
                                                  None, op0=ALU.mult))
                    v.wait_ge(ac.h, ac_sq(tt))
                    sv(v); dv.inc(v.tensor_scalar(stat[:, 3:4], stat[:, 2:3], 1.0 / D,
                                                  None, op0=ALU.mult))
                    sv(v); dv.inc(v.tensor_tensor(stat[:, 5:6], stat[:, 1:2],
                                                  stat[:, 1:2], op=ALU.mult))
                    sv(v); dv.inc(v.tensor_tensor(stat[:, 3:4], stat[:, 3:4],
                                                  stat[:, 5:6], op=ALU.subtract))
                    v.wait_ge(ac.h, ac_srt(tt))
                    sv(v); dv.inc(v.reciprocal(stat[:, 6:7], stat[:, 4:5]))
                    if tt >= 1:
                        v.wait_ge(pe.h, pe_tr(tt - 1, DC - 1))
                    sv(v); dv.inc(v.tensor_scalar(xln_tok[:], xb[tt % 2][:], stat[:, 1:2],
                                                  stat[:, 6:7], op0=ALU.subtract,
                                                  op1=ALU.mult))
                    for dc_ in range(DC):
                        v.wait_ge(pe.h, pe_tr(tt, dc_))
                        sl_ = (dc_ % 2) * 1024
                        sv(v); dv.inc(v.tensor_copy(
                            xlt[:, dc_ * xstride + tt * P:dc_ * xstride + (tt + 1) * P],
                            ps_tr[:, sl_:sl_ + P]))
            block.vector(f_vec)
            assert dv.n == dv0 + LNV * n_tiles, (dv.n, dv0, n_tiles)
            assert ac.n == ac0 + 2 * n_tiles
            assert pe.n == pe0 + DC * n_tiles
            return dict(dv_done=dv.n, dv_tile=dv_tile)

        with ExitStack() as ps3:
            asb = lambda n_, s_, d_: ps3.enter_context(nc.sbuf_tensor(n_, s_, d_))
            xb3 = [asb(f"a_x{i}", [P, D], F32) for i in range(2)]
            xsq3a = asb("a_xsq", [P, D], F32)
            xsq3 = [xsq3a, xsq3a]
            stat3 = asb("a_stat", [P, 8], F32)
            xlt3 = asb("a_xlt", [P, DC * TC], BF16)
            xln_tok3 = asb("a_xlntok", [P, D], BF16)
            wbig = asb("a_wbig", [P, DC * D], BF16)
            qev = asb("a_qev", [P, QW], BF16)
            kev = asb("a_kev", [P, D], BF16)
            zld = asb("a_zld", [P, 4 * NH], F32)

            with nc.psum_tensor("a_pstr", [P, 2048], BF16) as ps_tr3:
                ln3 = emit_ln("aln", NT,
                              lambda tt: x1_park[tt * P:(tt + 1) * P, :],
                              xlt3, TC, xb3, xsq3, stat3, xln_tok3, ps_tr3,
                              ffn_slots[0])
            dv_ln3 = ln3["dv_done"]

            W5 = min(512, D)
            PW = max(QW, W5, NH)
            with nc.psum_tensor("a_psq", [P, 2 * PW], F32) as ps_q:
                sg_q0 = sg.n

                def q_gp(g):
                    g.wait_ge(dv.h, dv_ln3)
                    sg.inc(g.dma_start(
                        out=wbig[:].rearrange("p (dc e) -> p dc e", dc=DC),
                        in_=wq_d.rearrange("(dc p) e -> p dc e", p=P)), 16)
                block.gpsimd(q_gp)
                peq0, acq0, sxq0 = pe.n, ac.n, sx.n
                NQ = DC * NQG

                def pe_q(k): return peq0 + k + 1
                def ac_q(k): return acq0 + k + 1
                def sx_q(k): return sxq0 + 16 * (QW // P) * (k + 1)

                def q_te(te):
                    te.wait_ge(sg.h, sg_q0 + 16)
                    te.wait_ge(dv.h, dv_ln3)
                    for ec in range(DC):
                        for n4 in range(NQG):
                            k = ec * NQG + n4
                            bank = k % 2
                            if k >= 2:
                                te.wait_ge(ac.h, ac_q(k - 2))
                            for dc_ in range(DC):
                                mm = te.matmul(
                                    ps_q[:, bank * PW:bank * PW + QW],
                                    wbig[:, dc_ * D + ec * P:dc_ * D + (ec + 1) * P],
                                    xlt3[:, dc_ * TC + n4 * QW:dc_ * TC + (n4 + 1) * QW],
                                    start=(dc_ == 0), stop=(dc_ == DC - 1))
                            pe.inc(mm)
                block.tensor(q_te)

                def q_sc(s):
                    for k in range(NQ):
                        s.wait_ge(pe.h, pe_q(k))
                        if k >= 1:
                            s.wait_ge(sx.h, sx_q(k - 1))
                        else:
                            s.wait_ge(sx.h, sxq0)
                        bank = k % 2
                        ss(s); ac.inc(s.activation(qev[:], ps_q[:, bank * PW:bank * PW + QW],
                                                   AF.Silu))
                block.scalar(q_sc)

                def q_store(s):
                    for ec in range(DC):
                        for n4 in range(NQG):
                            k = ec * NQG + n4
                            s.wait_ge(ac.h, ac_q(k))
                            for j in range(QW // P):
                                t_idx = n4 * (QW // P) + j
                                sx.inc(s.dma_start(
                                    out=q_park[t_idx * D + ec * P:t_idx * D + (ec + 1) * P, :],
                                    in_=qev[:, j * P:(j + 1) * P]), 16)
                block.sync(q_store)
                assert pe.n == peq0 + NQ and ac.n == acq0 + NQ
                assert sx.n == sxq0 + 16 * (QW // P) * NQ

                def kv_phase(wsrc, park, use_silu):
                    sgk0 = sg.n
                    pe_before = pe.n

                    def kv_gp(g):
                        g.wait_ge(pe.h, pe_before)
                        sg.inc(g.dma_start(
                            out=wbig[:].rearrange("p (dc e) -> p dc e", dc=DC),
                            in_=wsrc.rearrange("(dc p) e -> p dc e", p=P)), 16)
                    block.gpsimd(kv_gp)
                    pek0, ack0, sxk0 = pe.n, ac.n, sx.n

                    def pe_kv(k): return pek0 + k + 1
                    def ac_kv(k): return ack0 + k + 1
                    def sx_kv(tc_): return sxk0 + 16 * (tc_ + 1)

                    def kv_te(te):
                        te.wait_ge(sg.h, sgk0 + 16)
                        te.wait_ge(ac.h, ack0)
                        for tc_ in range(NT):
                            for es in range(NES):
                                k = tc_ * NES + es
                                bank = k % 2
                                if k >= 2:
                                    te.wait_ge(ac.h, ac_kv(k - 2))
                                for dc_ in range(DC):
                                    mm = te.matmul(
                                        ps_q[:, bank * PW:bank * PW + W5],
                                        xlt3[:, dc_ * TC + tc_ * P:dc_ * TC + (tc_ + 1) * P],
                                        wbig[:, dc_ * D + es * W5:dc_ * D + (es + 1) * W5],
                                        start=(dc_ == 0), stop=(dc_ == DC - 1))
                                pe.inc(mm)
                    block.tensor(kv_te)

                    def kv_sc(s):
                        for tc_ in range(NT):
                            for es in range(NES):
                                k = tc_ * NES + es
                                s.wait_ge(pe.h, pe_kv(k))
                                if tc_ >= 1 and es == 0:
                                    s.wait_ge(sx.h, sx_kv(tc_ - 1))
                                elif tc_ == 0 and es == 0:
                                    s.wait_ge(sx.h, sxk0)
                                bank = k % 2
                                fn = AF.Silu if use_silu else AF.Copy
                                ss(s); ac.inc(s.activation(
                                    kev[:, es * W5:(es + 1) * W5],
                                    ps_q[:, bank * PW:bank * PW + W5], fn))
                    block.scalar(kv_sc)

                    def kv_store(s):
                        for tc_ in range(NT):
                            s.wait_ge(ac.h, ac_kv(tc_ * NES + NES - 1))
                            sx.inc(s.dma_start(out=park[tc_ * P:(tc_ + 1) * P, :],
                                               in_=kev[:]), 16)
                    block.sync(kv_store)
                    assert pe.n == pek0 + NT * NES and ac.n == ack0 + NT * NES
                    assert sx.n == sxk0 + 16 * NT

                kv_phase(wk_d, k_park, True)
                kv_phase(wv_d, v_park, False)

                pel0, acl0, dvl0 = pe.n, ac.n, dv.n

                def pe_ld(tc_): return pel0 + tc_ + 1
                def ac_ld(tc_): return acl0 + 4 * (tc_ + 1)
                def dv_ld(tc_): return dvl0 + 2 * (tc_ + 1)

                def ld_te(te):
                    te.wait_ge(ac.h, acl0)
                    for tc_ in range(NT):
                        bank = tc_ % 2
                        if tc_ >= 2:
                            te.wait_ge(dv.h, dv_ld(tc_ - 2))
                            te.wait_ge(ac.h, ac_ld(tc_ - 2))
                        for dc_ in range(DC):
                            mm = te.matmul(
                                ps_q[:, bank * PW:bank * PW + NH],
                                xlt3[:, dc_ * TC + tc_ * P:dc_ * TC + (tc_ + 1) * P],
                                c_wf[:, dc_ * NH:(dc_ + 1) * NH],
                                start=(dc_ == 0), stop=(dc_ == DC - 1))
                        pe.inc(mm)
                block.tensor(ld_te)

                def ld_sc(s):
                    for tc_ in range(NT):
                        bank = tc_ % 2
                        s.wait_ge(pe.h, pe_ld(tc_))
                        if tc_ >= 1:
                            s.wait_ge(dv.h, dv_ld(tc_ - 1))
                        psl = ps_q[:, bank * PW:bank * PW + NH]
                        ss(s); ac.inc(s.activation(zld[:, 0:NH], psl, AF.Copy))
                        ss(s); ac.inc(s.activation(zld[:, NH:2 * NH], psl, AF.Abs))
                        ss(s); ac.inc(s.activation(zld[:, 2 * NH:3 * NH],
                                                   zld[:, NH:2 * NH], AF.Exp,
                                                   scale=ccol(COL_NEG1)))
                        ss(s); ac.inc(s.activation(zld[:, 3 * NH:4 * NH],
                                                   zld[:, 2 * NH:3 * NH], AF.Ln,
                                                   bias=ccol(COL_ONE)))
                block.scalar(ld_sc)

                def ld_vec(v):
                    for tc_ in range(NT):
                        v.wait_ge(ac.h, ac_ld(tc_))
                        sv(v); dv.inc(v.tensor_scalar_min(zld[:, 0:NH], zld[:, 0:NH], 0.0))
                        sv(v); dv.inc(v.tensor_tensor(ld_all[:, tc_ * NH:(tc_ + 1) * NH],
                                                      zld[:, 0:NH], zld[:, 3 * NH:4 * NH],
                                                      op=ALU.subtract))
                block.vector(ld_vec)
                assert pe.n == pel0 + NT and ac.n == acl0 + 4 * NT
                assert dv.n == dvl0 + 2 * NT
        barrier()
        if phases < 4:
            attn_stack.close()
            return nc

        o_all = attn_stack.enter_context(nc.sbuf_tensor("s_oall", [P, NT * D], BF16))
        scan_sbufs = ExitStack()
        ssb = lambda n_, s_, d_: scan_sbufs.enter_context(nc.sbuf_tensor(n_, s_, d_))
        D_sb2 = [ssb(f"s_D{i}", [P, D], F32) for i in range(2)]
        kT_sb2 = [ssb(f"s_kT{i}", [P, D], BF16) for i in range(2)]
        sh_sb2 = [ssb(f"s_sh{i}", [P, D], BF16) for i in range(2)]
        kd_sb2 = [ssb(f"s_kd{i}", [P, D], BF16) for i in range(2)]
        ktile = [ssb(f"s_kt{i}", [P, D], BF16) for i in range(2)]
        vtile = [ssb(f"s_vt{i}", [P, D], BF16) for i in range(2)]
        qtile = [ssb(f"s_qt{i}", [P, D], BF16) for i in range(2)]
        cum_tok2 = [ssb(f"s_cumtok{i}", [P, NH], F32) for i in range(2)]
        negcum2 = [ssb(f"s_negcum{i}", [P, NH], F32) for i in range(2)]
        cumT2 = [ssb(f"s_cumT{i}", [16, P], F32) for i in range(2)]
        cumrow2 = [ssb(f"s_cumrow{i}", [1, NH * P], F32) for i in range(2)]
        exptot_r2 = [ssb(f"s_exptotr{i}", [1, NH], F32) for i in range(2)]
        exptot_c2 = [ssb(f"s_exptotc{i}", [P, NH], F32) for i in range(2)]

        with scan_sbufs, \
             nc.psum_tensor("s_psA", [P, 512], F32) as ps_A, \
             nc.psum_tensor("s_psB", [P, HB * P], F32) as ps_B, \
             nc.psum_tensor("s_psS", [P, HGRP * P], F32) as ps_sT, \
             nc.psum_tensor("s_psO", [P, HGRP * P], F32) as ps_od, \
             nc.psum_tensor("s_psK", [P, 2048], BF16) as ps_kt:

            def s_init(v):
                sv(v); dv.inc(v.memset(S_sb[:], 0.0))
                sv(v); dv.inc(v.memset(S_bf[:], 0.0))
                sv(v); dv.inc(v.memset(Rtab[0:1, 0:NH], 0.0))
            block.vector(s_init)

            sxs0, sgs0, pes0, dvs0, acs0 = sx.n, sg.n, pe.n, dv.n, ac.n
            skv = [dmasem("skv0"), dmasem("skv1")]

            def kvq_done(m):
                return (skv[m % 2].h, 48 * (m // 2 + 1))
            PTILE = 3 + NCHB + NH + 4 * NG
            DTILE = 6 + 6 * NH
            ATILE = 2 + NH

            def sg_bn(m): return sgs0 + 32 * (m + 1)
            def pe_cum(m): return pes0 + PTILE * m + 1
            def pe_tot(m): return pes0 + PTILE * m + 2
            def pe_etc(m): return pes0 + PTILE * m + 3
            def pe_B(m, ch): return pes0 + PTILE * m + 3 + ch + 1
            def pe_kt(m, h): return pes0 + PTILE * m + 3 + NCHB + h + 1
            def pe_sc(m, g_): return pes0 + PTILE * m + 3 + NCHB + NH + 4 * g_ + 1
            def pe_oi(m, g_): return pes0 + PTILE * m + 3 + NCHB + NH + 4 * g_ + 2
            def pe_or(m, g_): return pes0 + PTILE * m + 3 + NCHB + NH + 4 * g_ + 3
            def pe_dS(m, g_): return pes0 + PTILE * m + 3 + NCHB + NH + 4 * g_ + 4
            def dv_cum(m): return dvs0 + DTILE * m + 3
            def dv_R(m): return dvs0 + DTILE * m + 4
            def dv_etc(m): return dvs0 + DTILE * m + 5
            def dv_kth(m): return dvs0 + DTILE * m + 5 + HGRP
            def dv_kt(m): return dvs0 + DTILE * m + 5 + NH
            def dv_sh(m, g_): return dvs0 + DTILE * m + 5 + NH + 5 * HGRP * g_ + 2 * HGRP
            def dv_ev1(m, g_): return dvs0 + DTILE * m + 5 + NH + 5 * HGRP * g_ + 3 * HGRP
            def dv_oev(m, g_): return dvs0 + DTILE * m + 5 + NH + 5 * HGRP * g_ + 4 * HGRP
            def dv_Su(m, g_): return dvs0 + DTILE * m + 5 + NH + 5 * HGRP * g_ + 5 * HGRP
            def dv_Sbf(m): return dvs0 + DTILE * (m + 1)
            def ac_ec(m): return acs0 + ATILE * m + 1
            def ac_etr(m): return acs0 + ATILE * m + 2
            def ac_D(m, ch): return acs0 + ATILE * m + 2 + (ch + 1) * HB

            def s_sync(s):
                for m in range(NT):
                    if m >= 2:
                        s.wait_ge(pe.h, pe_dS(m - 2, NG - 1))
                        s.wait_ge(dv.h, dv_Su(m - 2, NG - 1))
                    skv[m % 2].inc(s.dma_start(out=ktile[m % 2][:],
                                                in_=k_park[m * P:(m + 1) * P, :]), 16)
                    skv[m % 2].inc(s.dma_start(out=vtile[m % 2][:],
                                               in_=v_park[m * P:(m + 1) * P, :]), 16)
                    skv[m % 2].inc(s.dma_start(
                        out=qtile[m % 2][:].rearrange("p (ec t) -> p ec t", ec=DC),
                        in_=q_park[m * D:(m + 1) * D, :]
                            .rearrange("(ec p) t -> p ec t", p=P)), 16)
            block.sync(s_sync)

            def s_gp(g):
                for m in range(NT):
                    cumT = cumT2[m % 2]
                    cumrow = cumrow2[m % 2]
                    g.wait_ge(dv.h, dv_cum(m))
                    if m >= 2:
                        g.wait_ge(pe.h, pe_B(m - 2, NCHB - 1))
                    sg.inc(g.dma_start(
                        out=bounce[m:m + 1, :].rearrange("a (h t) -> (a h) t", h=NH),
                        in_=cumT[0:NH, :]), 16)
                    g.wait_ge(sg.h, sg.n)
                    sg.inc(g.dma_start(out=cumrow[:], in_=bounce[m:m + 1, :]), 16)
            block.gpsimd(s_gp)
            assert sg.n == sgs0 + 32 * NT

            def s_te(te):
                te.wait_ge(dv.h, dvs0)
                for m in range(NT):
                    cumT = cumT2[m % 2]
                    cumrow = cumrow2[m % 2]
                    exptot_r = exptot_r2[m % 2]
                    kT_sb = kT_sb2[m % 2]
                    sh_sb = sh_sb2[m % 2]
                    kd_sb = kd_sb2[m % 2]
                    if m >= 1:
                        te.wait_ge(dv.h, dv_etc(m - 1))
                        te.wait_ge(ac.h, ac_etr(m - 1))
                    mm = te.matmul(ps_A[:, 0:NH], c_triu[:], ld_all[:, m * NH:(m + 1) * NH],
                                   start=True, stop=True)
                    mm = te.matmul(ps_A[0:NH, 32:32 + P], ld_all[:, m * NH:(m + 1) * NH],
                                   c_triu[:], start=True, stop=True)
                    pe.inc(mm)
                    te.wait_ge(dv.h, dv_cum(m))
                    pe.inc(te.transpose(ps_A[0:1, 160:160 + NH], cumT[0:NH, P - 1:P],
                                        c_identf[0:NH, 0:NH]))
                    te.wait_ge(ac.h, ac_etr(m))
                    pe.inc(te.matmul(ps_A[:, 192:192 + NH], c_ones[:], exptot_r[:],
                                     start=True, stop=True))
                    for ch in range(NCHB):
                        te.wait_ge(sg.h, sg_bn(m))
                        if ch >= 1:
                            te.wait_ge(ac.h, ac_D(m, ch - 1))
                        elif m >= 1:
                            te.wait_ge(ac.h, ac_D(m - 1, NCHB - 1))
                        te.matmul(ps_B[:], c_ones[:],
                                  cumrow[:, ch * HB * P:(ch + 1) * HB * P],
                                  start=True, stop=False)
                        pe.inc(te.matmul(ps_B[:], c_identf[:], c_uxh[:],
                                         start=False, stop=True))
                    te.wait_ge(*kvq_done(m))
                    for h in range(NH):
                        sl = (h % 2) * 1024 + ((h % HGRP) // 2) * P
                        if h == HGRP and NG > 1:
                            te.wait_ge(dv.h, dv_kth(m))
                        elif h == 0 and m >= 1:
                            te.wait_ge(dv.h, dv_kt(m - 1))
                        if h >= 2:
                            te.wait_ge(dv.h, dvs0 + DTILE * m + 5 + (h - 2) + 1)
                        pe.inc(te.transpose(ps_kt[:, sl:sl + P],
                                            ktile[m % 2][:, h * P:(h + 1) * P],
                                            c_identb[:]))
                    for g_ in range(NG):
                        h0 = g_ * HGRP
                        te.wait_ge(dv.h, dv_kth(m) if g_ == 0 else dv_kt(m))
                        if g_ >= 1:
                            te.wait_ge(dv.h, dv_oev(m, g_ - 1))
                        elif m >= 1:
                            te.wait_ge(dv.h, dv_oev(m - 1, NG - 1))
                        for h in range(h0, h0 + HGRP):
                            sl = (h - h0) * P
                            mm = te.matmul(ps_sT[:, sl:sl + P],
                                           kT_sb[:, h * P:(h + 1) * P],
                                           qtile[m % 2][:, h * P:(h + 1) * P],
                                           start=True, stop=True)
                        pe.inc(mm)
                        te.wait_ge(dv.h, dv_sh(m, g_))
                        for h in range(h0, h0 + HGRP):
                            sl = (h - h0) * P
                            mm = te.matmul(ps_sT[:, sl:sl + P],
                                           sh_sb[:, h * P:(h + 1) * P],
                                           vtile[m % 2][:, h * P:(h + 1) * P],
                                           start=True, stop=True)
                        pe.inc(mm)
                        if m >= 1:
                            te.wait_ge(dv.h, dv_Sbf(m - 1))
                        if g_ >= 1:
                            te.wait_ge(dv.h, dv_Su(m, g_ - 1))
                        elif m >= 1:
                            te.wait_ge(dv.h, dv_Su(m - 1, NG - 1))
                        for h in range(h0, h0 + HGRP):
                            sl = (h - h0) * P
                            mm = te.matmul(ps_od[:, sl:sl + P],
                                           qtile[m % 2][:, h * P:(h + 1) * P],
                                           S_bf[:, h * P:(h + 1) * P],
                                           start=True, stop=True)
                        pe.inc(mm)
                        te.wait_ge(dv.h, dv_ev1(m, g_))
                        for h in range(h0, h0 + HGRP):
                            sl = (h - h0) * P
                            mm = te.matmul(ps_od[:, sl:sl + P],
                                           kd_sb[:, h * P:(h + 1) * P],
                                           vtile[m % 2][:, h * P:(h + 1) * P],
                                           start=True, stop=True)
                        pe.inc(mm)
            block.tensor(s_te)
            assert pe.n == pes0 + PTILE * NT

            def s_vec(v):
                for m in range(NT):
                    cum_tok = cum_tok2[m % 2]
                    negcum = negcum2[m % 2]
                    cumT = cumT2[m % 2]
                    exptot_c = exptot_c2[m % 2]
                    D_sb = D_sb2[m % 2]
                    kT_sb = kT_sb2[m % 2]
                    sh_sb = sh_sb2[m % 2]
                    kd_sb = kd_sb2[m % 2]
                    v.wait_ge(pe.h, pe_cum(m))
                    if m >= 2:
                        v.wait_ge(ac.h, ac_D(m - 2, NCHB - 1))
                        v.wait_ge(sg.h, sg_bn(m - 2))
                    sv(v); dv.inc(v.tensor_copy(cum_tok[:], ps_A[:, 0:NH]))
                    sv(v); dv.inc(v.tensor_copy(cumT[0:NH, :], ps_A[0:NH, 32:32 + P]))
                    sv(v); dv.inc(v.tensor_scalar(negcum[:], cum_tok[:], -1.0, None,
                                                  op0=ALU.mult))
                    v.wait_ge(pe.h, pe_tot(m))
                    sv(v); dv.inc(v.tensor_tensor(Rtab[0:1, (m + 1) * NH:(m + 2) * NH],
                                                  Rtab[0:1, m * NH:(m + 1) * NH],
                                                  ps_A[0:1, 160:160 + NH], op=ALU.add))
                    v.wait_ge(pe.h, pe_etc(m))
                    sv(v); dv.inc(v.tensor_copy(exptot_c[:], ps_A[:, 192:192 + NH]))
                    for h in range(NH):
                        v.wait_ge(pe.h, pe_kt(m, h))
                        sl = (h % 2) * 1024 + ((h % HGRP) // 2) * P
                        sv(v); dv.inc(v.tensor_copy(kT_sb[:, h * P:(h + 1) * P],
                                                    ps_kt[:, sl:sl + P]))
                    for g_ in range(NG):
                        h0 = g_ * HGRP
                        v.wait_ge(pe.h, pe_sc(m, g_))
                        for h in range(h0, h0 + HGRP):
                            v.wait_ge(ac.h, ac_D(m, h // HB))
                            sl = (h - h0) * P
                            sv(v); dv.inc(v.tensor_tensor(sh_sb[:, h * P:(h + 1) * P],
                                                          ps_sT[:, sl:sl + P],
                                                          D_sb[:, h * P:(h + 1) * P],
                                                          op=ALU.mult))
                            sv(v); dv.inc(v.tensor_scalar_mul(
                                kd_sb[:, h * P:(h + 1) * P],
                                ktile[m % 2][:, h * P:(h + 1) * P],
                                D_sb[:, h * P + P - 1:h * P + P]))
                        v.wait_ge(pe.h, pe_or(m, g_))
                        v.wait_ge(ac.h, ac_ec(m))
                        for h in range(h0, h0 + HGRP):
                            sl = (h - h0) * P
                            sv(v); dv.inc(v.tensor_scalar_mul(
                                o_all[:, m * D + h * P:m * D + (h + 1) * P],
                                ps_od[:, sl:sl + P],
                                expcum_all[:, m * NH + h:m * NH + h + 1]))
                        v.wait_ge(pe.h, pe_oi(m, g_))
                        for h in range(h0, h0 + HGRP):
                            sl = (h - h0) * P
                            sv(v); dv.inc(v.tensor_tensor(
                                o_all[:, m * D + h * P:m * D + (h + 1) * P],
                                ps_sT[:, sl:sl + P],
                                o_all[:, m * D + h * P:m * D + (h + 1) * P],
                                op=ALU.add))
                        v.wait_ge(pe.h, pe_dS(m, g_))
                        for h in range(h0, h0 + HGRP):
                            sl = (h - h0) * P
                            sv(v); dv.inc(v.scalar_tensor_tensor(
                                S_sb[:, h * P:(h + 1) * P], S_sb[:, h * P:(h + 1) * P],
                                exptot_c[:, h:h + 1], ps_od[:, sl:sl + P],
                                op0=ALU.mult, op1=ALU.add))
                    v.wait_ge(pe.h, pe_or(m, NG - 1))
                    sv(v); dv.inc(v.tensor_copy(S_bf[:], S_sb[:]))
            block.vector(s_vec)
            assert dv.n == dvs0 + DTILE * NT

            def s_sc(s):
                for m in range(NT):
                    cum_tok = cum_tok2[m % 2]
                    negcum = negcum2[m % 2]
                    exptot_r = exptot_r2[m % 2]
                    D_sb = D_sb2[m % 2]
                    s.wait_ge(dv.h, dv_cum(m))
                    ss(s); ac.inc(s.activation(expcum_all[:, m * NH:(m + 1) * NH],
                                               cum_tok[:], AF.Exp))
                    s.wait_ge(dv.h, dv_R(m))
                    if m >= 2:
                        s.wait_ge(pe.h, pe_etc(m - 2))
                    ss(s); ac.inc(s.activation(exptot_r[:], ps_A[0:1, 160:160 + NH],
                                               AF.Exp))
                    for ch in range(NCHB):
                        s.wait_ge(pe.h, pe_B(m, ch))
                        if m >= 2 and ch == 0:
                            s.wait_ge(dv.h, dv_sh(m - 2, NG - 1))
                        for hh in range(HB):
                            h = ch * HB + hh
                            ss(s); ac.inc(s.activation(D_sb[:, h * P:(h + 1) * P],
                                                       ps_B[:, hh * P:(hh + 1) * P],
                                                       AF.Exp, bias=negcum[:, h:h + 1]))
            block.scalar(s_sc)
            assert ac.n == acs0 + ATILE * NT

            gq_sr0 = gq.n

            def s_pub(g):
                g.wait_ge(dv.h, dv_Sbf(NT - 1))
                with g.register("par_s") as par, g.register("want_s") as want, \
                     g.register("seen_s") as seen, g.register("more_s") as more, \
                     g.register("tgt_s") as tgt:
                    g.load(par, parity_sb[0:1, 0:1])
                    g.load(want, nonce_sb[0:1, 0:1])
                    branch_dma(g, par, sg, smbox[0:P, :], smbox[P:2 * P, :], S_sb[:])
                    g.wait_ge(sg.h, sg.n)
                    branch_dma(g, par, sg, flags_s[FLG_S0:FLG_S0 + 1, :],
                               flags_s[FLG_S1:FLG_S1 + 1, :], flagrow[:])
                    g.wait_ge(sg.h, sg.n)
                    g.reg_mov(more, 1)
                    g.reg_mov(tgt, 0)
                    with g.While(more):
                        g.reg_add(tgt, tgt, 16)
                        if solo:
                            g.dma_start(out=flagrd[:],
                                        in_=flags_s[FLG_S1:FLG_S1 + 1, :])\
                                .then_inc(spoll2.h, 16)
                        else:
                            with g.If_eq(par, 0):
                                g.dma_start(out=flagrd[:],
                                            in_=flags_s[FLG_S1:FLG_S1 + 1, :])\
                                    .then_inc(spoll2.h, 16)
                            with g.Else():
                                g.dma_start(out=flagrd[:],
                                            in_=flags_s[FLG_S0:FLG_S0 + 1, :])\
                                    .then_inc(spoll2.h, 16)
                        g.wait_ge(spoll2.h, tgt)
                        g.load(seen, flagrd[0:1, 0:1])
                        g.reg_alu(more, seen, want, ALU.not_equal)
                    if solo:
                        sg.inc(g.dma_start(out=S_sb[:], in_=smbox[0:P, :]), 16)
                    else:
                        branch_dma(g, par, sg, S_sb[:], S_sb[:],
                                   smbox[P:2 * P, :], smbox[0:P, :])
                    g.wait_ge(sg.h, sg.n)
                gq.inc(g.nop())
            block.gpsimd(s_pub)
            gq_srecv = gq.n

            def s_mask(v):
                v.wait_ge(gq.h, gq_srecv)
                sv(v); dv.inc(v.tensor_scalar_mul(SR_bf[:], S_sb[:], ccol(COL_MASK)))
            block.vector(s_mask)
            dv_srbf = dv.n
        barrier()
        if phases < 5:
            attn_stack.close()
            return nc

        with ExitStack() as p5:
            csb = lambda n_, s_, d_: p5.enter_context(nc.sbuf_tensor(n_, s_, d_))
            wo_bf = csb("c_wobf", [P, DC * D], BF16)
            normw_row = csb("c_normwrow", [1, D], F32)
            normw_b = csb("c_normwb", [P, D], BF16)
            o_f32 = csb("c_of32", [P, D], F32)
            cstat = csb("c_cstat", [P, 4], F32)
            expR = csb("c_expR", [1, NH], F32)
            colsc = csb("c_colsc", [P, NH], F32)
            on_bf = csb("c_onbf", [P, D], BF16)
            ont = csb("c_ont", [P, D], BF16)
            x1t1 = csb("c_x1t", [P, D], F32)
            x1t = [x1t1, x1t1]
            qt5 = [csb(f"c_qt5{i}", [P, D], BF16) for i in range(2)]
            x2t1 = csb("c_x2t", [P, D], F32)
            x2t = [x2t1, x2t1]


            with nc.psum_tensor("c_psc", [P, D], F32) as ps_c, \
                 nc.psum_tensor("c_psw", [P, 1024], F32) as ps_w, \
                 nc.psum_tensor("c_pst", [P, 2048], BF16) as ps_t5:

                sg5a = sg.n

                def c_gp0(g):
                    sg.inc(g.dma_start(
                        out=wo_bf[:].rearrange("p (fc e) -> p fc e", fc=DC),
                        in_=wo_d.rearrange("(fc p) e -> p fc e", p=P)), 16)
                block.gpsimd(c_gp0)
                sx5a = sx.n

                def c_sync0(s):
                    sx.inc(s.dma_start(out=normw_row[:], in_=normw_d), 16)
                block.sync(c_sync0)

                pe5a = pe.n

                dv5a = dv.n

                def c_te0(te):
                    te.wait_ge(sx.h, sx5a + 16)
                    for j in range(NDS):
                        if j >= 1:
                            te.wait_ge(dv.h, dv5a + j)
                        pe.inc(te.matmul(ps_w[:, 0:512], c_ones[:],
                                         normw_row[0:1, j * 512:(j + 1) * 512],
                                         start=True, stop=True))
                block.tensor(c_te0)

                def c_vec0(v):
                    for j in range(NDS):
                        v.wait_ge(pe.h, pe5a + j + 1)
                        sv(v); dv.inc(v.tensor_copy(normw_b[:, j * 512:(j + 1) * 512],
                                                    ps_w[:, 0:512]))
                block.vector(c_vec0)
                dv_nb = dv.n

                sx50, sg50, pe50, dv50, ac50 = sx.n, sg.n, pe.n, dv.n, ac.n
                s15 = [dmasem("s150"), dmasem("s151")]
                sx1 = dmasem("sx1")
                s2one = dmasem("s2one")
                s2o0 = s2one.n
                sx10 = sx1.n

                def xq5_done(m):
                    return (s15[m % 2].h, 16 * (m // 2 + 1))

                def x1_done(m):
                    return (sx1.h, sx10 + 16 * (m + 1))

                def x2_free(m):
                    return (s2one.h, s2o0 + 16 * m)
                PT5 = 1 + 1 + DC + NDS
                DT5 = 2 + NH + 1 + 1 + DC + NDS
                AT5 = 3


                def ac_exR(m): return ac50 + AT5 * m + 1
                def ac_sq(m): return ac50 + AT5 * m + 2
                def ac_srt(m): return ac50 + AT5 * m + 3
                def pe_msk(m): return pe50 + PT5 * m + 1
                def pe_corr(m): return pe50 + PT5 * m + 2
                def pe_tr5(m, dc_): return pe50 + PT5 * m + 2 + dc_ + 1
                def pe_wo(m, ds_): return pe50 + PT5 * m + 2 + DC + ds_ + 1
                def dv_colsc(m): return dv50 + DT5 * m + 1
                def dv_of(m): return dv50 + DT5 * m + 2
                def dv_stt(m): return dv50 + DT5 * m + 2 + NH
                def dv_rstd(m): return dv50 + DT5 * m + 3 + NH
                def dv_on(m): return dv50 + DT5 * m + 4 + NH
                def dv_trev(m, dc_): return dv50 + DT5 * m + 4 + NH + dc_ + 1
                def dv_trall(m): return dv50 + DT5 * m + 4 + NH + DC
                def dv_woev(m, ds_): return dv50 + DT5 * m + 4 + NH + DC + ds_ + 1
                def dv_x2(m): return dv50 + DT5 * (m + 1)

                def c_sync(s):
                    for m in range(NT):
                        if m >= 1:
                            s.wait_ge(dv.h, dv_woev(m - 1, NDS - 1))
                        sx1.inc(s.dma_start(out=x1t[m % 2][:],
                                            in_=x1_park[m * P:(m + 1) * P, :]), 16)
                        if m >= 2:
                            s.wait_ge(pe.h, pe_corr(m - 2))
                        s15[m % 2].inc(s.dma_start(
                            out=qt5[m % 2][:].rearrange("p (ec t) -> p ec t", ec=DC),
                            in_=q_park[m * D:(m + 1) * D, :]
                                .rearrange("(ec p) t -> p ec t", p=P)), 16)
                block.sync(c_sync)

                def c_sc(s):
                    for m in range(NT):
                        if m >= 1:
                            s.wait_ge(pe.h, pe_msk(m - 1))
                        ss(s); ac.inc(s.activation(expR[:], Rtab[0:1, m * NH:(m + 1) * NH],
                                                   AF.Exp))
                        s.wait_ge(dv.h, dv_stt(m))
                        if m >= 1:
                            s.wait_ge(pe.h, pe_wo(m - 1, NDS - 1))
                        ss(s); ac.inc(s.activation(ont[:], o_f32[:], AF.Square,
                                                   accum_out=cstat[:, 0:1]))
                        ss(s); ac.inc(s.activation(cstat[:, 1:2], cstat[:, 0:1], AF.Sqrt,
                                                   scale=ccol(COL_INVD),
                                                   bias=ccol(COL_RMSEPS)))
                block.scalar(c_sc)

                def c_te(te):
                    for m in range(NT):
                        te.wait_ge(ac.h, ac_exR(m))
                        if m >= 1:
                            te.wait_ge(dv.h, dv_colsc(m - 1))
                        else:
                            te.wait_ge(dv.h, dv_nb)
                        if m >= 1:
                            te.wait_ge(dv.h, dv_woev(m - 1, max(0, NDS - 2)))
                        pe.inc(te.matmul(ps_w[:, 0:NH], c_mask[:], expR[:],
                                         start=True, stop=True))
                        te.wait_ge(*xq5_done(m))
                        te.wait_ge(dv.h, dv_srbf)
                        if m >= 1:
                            te.wait_ge(dv.h, dv_stt(m - 1))
                        for h in range(NH):
                            mm = te.matmul(ps_c[:, h * P:(h + 1) * P],
                                           qt5[m % 2][:, h * P:(h + 1) * P],
                                           SR_bf[:, h * P:(h + 1) * P],
                                           start=True, stop=True)
                        pe.inc(mm)
                        te.wait_ge(dv.h, dv_on(m))
                        for dc_ in range(DC):
                            if dc_ >= 2:
                                te.wait_ge(dv.h, dv_trev(m, dc_ - 2))
                            elif m >= 1:
                                te.wait_ge(dv.h, dv_trall(m - 1))
                            sl5 = (dc_ % 2) * 1024
                            pe.inc(te.transpose(ps_t5[:, sl5:sl5 + P],
                                                on_bf[:, dc_ * P:(dc_ + 1) * P],
                                                c_identb[:]))
                        te.wait_ge(sg.h, sg5a + 16)
                        te.wait_ge(dv.h, dv_trall(m))
                        for ds_ in range(NDS):
                            bank = ds_ % 2
                            if ds_ == 0:
                                te.wait_ge(dv.h, dv_colsc(m))
                            if ds_ >= 2:
                                te.wait_ge(dv.h, dv_woev(m, ds_ - 2))
                            elif m >= 1:
                                te.wait_ge(dv.h, dv_woev(m - 1, NDS - 2 + ds_))
                            for fc in range(DC):
                                mm = te.matmul(
                                    ps_w[:, bank * 512:(bank + 1) * 512],
                                    ont[:, fc * P:(fc + 1) * P],
                                    wo_bf[:, fc * D + ds_ * 512:fc * D + (ds_ + 1) * 512],
                                    start=(fc == 0), stop=(fc == DC - 1))
                            pe.inc(mm)
                block.tensor(c_te)
                assert pe.n == pe50 + PT5 * NT

                def c_vec(v):
                    for m in range(NT):
                        v.wait_ge(pe.h, pe_msk(m))
                        sv(v); dv.inc(v.tensor_tensor(colsc[:],
                                                      expcum_all[:, m * NH:(m + 1) * NH],
                                                      ps_w[:, 0:NH], op=ALU.mult))
                        sv(v); dv.inc(v.tensor_copy(o_f32[:], o_all[:, m * D:(m + 1) * D]))
                        v.wait_ge(pe.h, pe_corr(m))
                        for h in range(NH):
                            sv(v); dv.inc(v.scalar_tensor_tensor(
                                o_f32[:, h * P:(h + 1) * P], ps_c[:, h * P:(h + 1) * P],
                                colsc[:, h:h + 1], o_f32[:, h * P:(h + 1) * P],
                                op0=ALU.mult, op1=ALU.add))
                        v.wait_ge(ac.h, ac_srt(m))
                        sv(v); dv.inc(v.reciprocal(cstat[:, 2:3], cstat[:, 1:2]))
                        v.wait_ge(dv.h, dv_nb)
                        sv(v); dv.inc(v.scalar_tensor_tensor(on_bf[:], o_f32[:],
                                                             cstat[:, 2:3], normw_b[:],
                                                             op0=ALU.mult, op1=ALU.mult))
                        for dc_ in range(DC):
                            v.wait_ge(pe.h, pe_tr5(m, dc_))
                            sl5 = (dc_ % 2) * 1024
                            sv(v); dv.inc(v.tensor_copy(ont[:, dc_ * P:(dc_ + 1) * P],
                                                        ps_t5[:, sl5:sl5 + P]))
                        v.wait_ge(*x1_done(m))
                        for ds_ in range(NDS):
                            bank = ds_ % 2
                            v.wait_ge(pe.h, pe_wo(m, ds_))
                            if m >= 1:
                                v.wait_ge(*x2_free(m))
                            sv(v); dv.inc(v.tensor_tensor(
                                x2t[m % 2][:, ds_ * 512:(ds_ + 1) * 512],
                                ps_w[:, bank * 512:(bank + 1) * 512],
                                x1t[m % 2][:, ds_ * 512:(ds_ + 1) * 512], op=ALU.add))
                block.vector(c_vec)
                assert dv.n == dv50 + DT5 * NT

                def c_store(g):
                    for m in range(NT):
                        g.wait_ge(dv.h, dv_x2(m))
                        s2one.inc(g.dma_start(out=x2_park[m * P:(m + 1) * P, :],
                                              in_=x2t[m % 2][:]), 16)
                    g.wait_ge(s2one.h, s2one.n)
                block.gpsimd(c_store)
        attn_stack.close()
        barrier()
        if phases < 6:
            return nc

        with ExitStack() as f2stk:
            ps_f2 = [f2stk.enter_context(nc.psum_tensor(f"f2_ps{i}", [P, 512], F32))
                     for i in range(max(4, NGD * NDS))]
            ffn2 = make_ffn("f2", x2_park, y_out, False, f2stk, ps_f2)
            ffn2["prologue"](fence=[(phb.h, N_ENG * phase_i[0])])
            for half in range(c.NHALF):
                ffn2["template"](half)

        def fin_sync(s):
            s.wait_ge(sx.h, sx.n)
        block.sync(fin_sync)

        def fin_gp(g):
            g.wait_ge(sg.h, sg.n)
        block.gpsimd(fin_gp)

    return nc



def _make_consts(cfg, par):
    a = np.zeros((P, 8), np.float32)
    a[:, 0] = LN_EPS
    a[:, 1] = RMS_EPS
    a[:, 2] = 1.0
    a[:, 3] = -1.0
    a[:, 4] = 1.0 / cfg.D
    a[:, 5] = 0.5
    a[:, 6] = 0.0
    a[:, 7] = 1.0 if par == 1 else 0.0
    return a


def make_in_maps(cfg, inputs, nonce_val, n_cores=8):
    c = cfg
    x = np.asarray(inputs["L0_attn_x"])
    Wq = np.asarray(inputs["L0_attn_Wq"], np.float32)
    Wk = np.asarray(inputs["L0_attn_Wk"], np.float32)
    Wv = np.asarray(inputs["L0_attn_Wv"], np.float32)
    Wf = np.asarray(inputs["L0_attn_Wf"], np.float32)
    Wo = np.asarray(inputs["L0_attn_Wo"], np.float32)
    nw = np.asarray(inputs["L0_attn_norm_w"], np.float32)
    lut = np.asarray(inputs["L0_ffn_lut"], np.float32)
    gidx = np.asarray(inputs["L0_ffn_gate_idx"], np.int32)
    uidx = np.asarray(inputs["L0_ffn_up_idx"], np.int32)
    didx = np.asarray(inputs["L0_ffn_down_idx"], np.int32)
    D = c.D
    OCHh = c.OCH
    NRBF = (OCHh + 1) // 2
    IHE = NRBF * P
    gT = np.ascontiguousarray(gidx.T)
    uT = np.ascontiguousarray(uidx.T)
    dT = np.ascontiguousarray(didx.T)

    def blocked(mT):
        return np.ascontiguousarray(
            mT.reshape(D, c.OCH, P).transpose(1, 0, 2).reshape(c.OCH * D, P))

    gB, uB = blocked(gT), blocked(uT)
    HB_ = 4
    triu = np.triu(np.ones((P, P))).astype(np.float32)
    identb = np.eye(P).astype(ml_dtypes.bfloat16)
    identf = np.eye(P).astype(np.float32)
    uxh = np.tile(np.tril(np.ones((P, P), np.float32), -1) * BIGNEG, (1, HB_))
    ones_row = np.ones((1, P), np.float32)

    in_maps = []
    for core in range(n_cores):
        b, half = core // 2, core % 2
        par = core % 2
        in_maps.append({
            "x_in": np.ascontiguousarray(x[b, half * c.TC:(half + 1) * c.TC, :],
                                         dtype=np.float32),
            "wq": Wq, "wk": Wk, "wv": Wv, "wo": Wo, "wf": Wf,
            "normw": np.ascontiguousarray(nw.reshape(1, D)),
            "lutrow": np.ascontiguousarray(lut.reshape(1, 16)),
            "mata_idx": (gB if par == 0 else uB),
            "matb_idx": (np.ascontiguousarray(dT[0:IHE, :]) if par == 0 else
                         np.ascontiguousarray(
                             np.vstack([dT[IHE:], dT[-(2 * IHE - c.INTER):]])
                             if 2 * IHE > c.INTER else dT[IHE:2 * IHE])),
            "triu": triu, "identb": identb, "identf": identf, "uxh": uxh,
            "onesrow": ones_row,
            "maskrow": np.full((1, P), 1.0 if par == 1 else 0.0, np.float32),
            "consts": _make_consts(c, par),
            "parity": np.array([[par]], np.int32),
            "nonce": np.array([[nonce_val]], np.int32),
        })
    return in_maps


_nc_cache = {}


def kernel(L0_attn_x, L0_attn_Wq, L0_attn_Wk, L0_attn_Wv, L0_attn_Wf, L0_attn_Wo,
           L0_attn_norm_w, L0_ffn_lut, L0_ffn_gate_idx, L0_ffn_up_idx, L0_ffn_down_idx):
    cfg = Cfg()
    nonce_val = int(np.random.randint(1, 2 ** 24))
    inputs = dict(L0_attn_x=L0_attn_x, L0_attn_Wq=L0_attn_Wq, L0_attn_Wk=L0_attn_Wk,
                  L0_attn_Wv=L0_attn_Wv, L0_attn_Wf=L0_attn_Wf, L0_attn_Wo=L0_attn_Wo,
                  L0_attn_norm_w=L0_attn_norm_w, L0_ffn_lut=L0_ffn_lut,
                  L0_ffn_gate_idx=L0_ffn_gate_idx, L0_ffn_up_idx=L0_ffn_up_idx,
                  L0_ffn_down_idx=L0_ffn_down_idx)
    in_maps = make_in_maps(cfg, inputs, nonce_val)
    if "nc" not in _nc_cache:
        _nc_cache["nc"] = build_program(cfg)
    nc = _nc_cache["nc"]
    res = run_bass_kernel_spmd(nc, in_maps, core_ids=list(range(8)))
    B, N, D = L0_attn_x.shape
    y = np.empty((B, N, D), np.float32)
    for core in range(8):
        b, half = core // 2, core % 2
        y[b, half * cfg.TC:(half + 1) * cfg.TC, :] = res.results[core]["y_out"]
    return y


# version 08
# speedup vs baseline: 1.3035x; 1.3035x over previous
import numpy as np
import ml_dtypes
from contextlib import ExitStack
from dataclasses import dataclass

import concourse.bass as bass
import concourse.mybir as mybir
from concourse.bass_utils import run_bass_kernel_spmd

F32 = mybir.dt.float32
BF16 = mybir.dt.bfloat16
I32 = mybir.dt.int32
AF = mybir.ActivationFunctionType
ALU = mybir.AluOpType
AX = mybir.AxisListType

P = 128
BIGNEG = -1.0e9
LN_EPS = 1e-5
RMS_EPS = 1e-6


@dataclass
class Cfg:
    D: int = 2048
    INTER: int = 5504
    TC: int = 2048
    NH: int = 16
    TF: int = 512
    OSG: int = 1
    TG: int = 256
    QW: int = 512
    hgrp: int = 0

    @property
    def DC(self): return self.D // P
    @property
    def OCH(self): return self.INTER // P
    @property
    def NT(self): return self.TC // P
    @property
    def NHALF(self): return self.TC // self.TF
    @property
    def NFLG(self): return (self.OCH + self.OSG - 1) // self.OSG


class SemCtr:
    def __init__(self, nc, stack, name):
        self.h = stack.enter_context(nc.semaphore(name))
        self.n = 0

    def inc(self, instr, k=1):
        instr.then_inc(self.h, k)
        self.n += k
        return instr


def build_program(cfg: Cfg, phases=99, solo=False, detect_races=True, ffn_parts=99):
    c = cfg
    D, INTER, TC, NH = c.D, c.INTER, c.TC, c.NH
    DC, OCH, NT, TF, TG = c.DC, c.OCH, c.NT, c.TF, c.TG
    QW = min(c.QW, TC)
    NQG = TC // QW
    N512 = min(512, TC)
    NES = D // min(512, D)
    NDS = D // min(512, D)
    NGD = TG // P
    NTT = TF // P
    NTG = TF // TG
    NFLG = c.NFLG
    NRBF = (OCH + 1) // 2
    IHE = NRBF * P
    HB = 4
    HGRP = c.hgrp if c.hgrp else min(8, NH)
    NG = NH // HGRP
    NCHB = NH // HB
    GW = TF // 512 if TF >= 512 else 1
    GWW = min(512, TF)

    nc = bass.Bass("TRN2", target_bir_lowering=False,
                   detect_race_conditions=detect_races)
    ap = lambda t: t.ap()
    x_in = ap(nc.dram_tensor("x_in", [TC, D], F32, kind="ExternalInput"))
    wq_d = ap(nc.dram_tensor("wq", [D, D], F32, kind="ExternalInput"))
    wk_d = ap(nc.dram_tensor("wk", [D, D], F32, kind="ExternalInput"))
    wv_d = ap(nc.dram_tensor("wv", [D, D], F32, kind="ExternalInput"))
    wo_d = ap(nc.dram_tensor("wo", [D, D], F32, kind="ExternalInput"))
    wf_d = ap(nc.dram_tensor("wf", [D, NH], F32, kind="ExternalInput"))
    normw_d = ap(nc.dram_tensor("normw", [1, D], F32, kind="ExternalInput"))
    lut_d = ap(nc.dram_tensor("lutrow", [1, 16], F32, kind="ExternalInput"))
    mata_d = ap(nc.dram_tensor("mata_idx", [OCH * D, P], I32, kind="ExternalInput"))
    matb_d = ap(nc.dram_tensor("matb_idx", [NRBF * P, D], I32, kind="ExternalInput"))
    triu_d = ap(nc.dram_tensor("triu", [P, P], F32, kind="ExternalInput"))
    identb_d = ap(nc.dram_tensor("identb", [P, P], BF16, kind="ExternalInput"))
    identf_d = ap(nc.dram_tensor("identf", [P, P], F32, kind="ExternalInput"))
    uxh_d = ap(nc.dram_tensor("uxh", [P, HB * P], F32, kind="ExternalInput"))
    ones_d = ap(nc.dram_tensor("onesrow", [1, P], F32, kind="ExternalInput"))
    mask_d = ap(nc.dram_tensor("maskrow", [1, P], F32, kind="ExternalInput"))
    consts_d = ap(nc.dram_tensor("consts", [P, 8], F32, kind="ExternalInput"))
    parity_d = ap(nc.dram_tensor("parity", [1, 1], I32, kind="ExternalInput"))
    nonce_d = ap(nc.dram_tensor("nonce", [1, 1], I32, kind="ExternalInput"))

    y_out = ap(nc.dram_tensor("y_out", [TC, D], F32, kind="ExternalOutput"))

    x1_park = ap(nc.dram_tensor("x1_park", [TC, D], F32, kind="Internal"))
    x2_park = ap(nc.dram_tensor("x2_park", [TC, D], F32, kind="Internal"))
    q_park = ap(nc.dram_tensor("q_park", [NT * D, P], BF16, kind="Internal"))
    k_park = ap(nc.dram_tensor("k_park", [TC, D], BF16, kind="Internal"))
    v_park = ap(nc.dram_tensor("v_park", [TC, D], BF16, kind="Internal"))
    bounce = ap(nc.dram_tensor("bounce", [NT, NH * P], F32, kind="Internal"))

    shr = dict(kind="Internal") if solo else dict(kind="Internal", addr_space="Shared")
    wgt_s = ap(nc.dram_tensor("wgt_s", [D, INTER], BF16, **shr))
    wut_s = ap(nc.dram_tensor("wut_s", [D, INTER], BF16, **shr))
    wdt_s = ap(nc.dram_tensor("wdt_s", [2 * IHE, D], BF16, **shr))
    smbox = ap(nc.dram_tensor("smbox", [2 * P, D], F32, **shr))
    WDE0 = 2 * NFLG
    WDO0 = 2 * NFLG + NRBF
    FLG_S0 = 2 * NFLG + 2 * NRBF
    FLG_S1 = FLG_S0 + 1
    NFLAGS = FLG_S1 + 1
    flags_s = ap(nc.dram_tensor("flags_s", [NFLAGS, P], I32, **shr))

    with ExitStack() as top:
        sb = lambda n_, s_, d_: top.enter_context(nc.sbuf_tensor(n_, s_, d_))

        c_triu = sb("c_triu", [P, P], F32)
        c_identb = sb("c_identb", [P, P], BF16)
        c_identf = sb("c_identf", [P, P], F32)
        c_uxh = sb("c_uxh", [P, HB * P], F32)
        c_ones = sb("c_ones", [1, P], F32)
        c_mask = sb("c_mask", [1, P], F32)
        c_consts = sb("c_consts", [P, 8], F32)
        c_lutcols = sb("c_lutcols", [P, 16], F32)
        c_lutrow = sb("c_lutrow", [1, 16], F32)
        c_wf = sb("c_wf", [P, DC * NH], BF16)
        nonce_sb = sb("nonce_sb", [1, 1], I32)
        parity_sb = sb("parity_sb", [1, 1], I32)
        flagrow = sb("flagrow", [1, P], I32)
        flagrd = sb("flagrd", [1, P], I32)

        (COL_LNEPS, COL_RMSEPS, COL_ONE, COL_NEG1, COL_INVD, COL_HALF, COL_ZERO,
         COL_MASK) = range(8)
        ccol = lambda i: c_consts[:, i:i + 1]

        dma_sems = []

        def dmasem(name):
            s_ = SemCtr(nc, top, name)
            dma_sems.append(s_)
            return s_

        sx = SemCtr(nc, top, "sx")
        sg = SemCtr(nc, top, "sg")
        sld = SemCtr(nc, top, "sld")
        spoll = SemCtr(nc, top, "spoll")
        spoll2 = SemCtr(nc, top, "spoll2")
        pe = SemCtr(nc, top, "pe")
        dv = SemCtr(nc, top, "dv")
        ac = SemCtr(nc, top, "ac")
        gq = SemCtr(nc, top, "gq")
        phb = SemCtr(nc, top, "phb")

        def sv(v):
            v.wait_ge(dv.h, dv.n)

        def ss(s):
            s.wait_ge(ac.h, ac.n)

        block = top.enter_context(nc.Block())
        if getattr(build_program, "bodymap", None) is not None:
            _records = build_program.bodymap
            _raw = block

            class _BlockProxy:
                def __getattr__(self, name):
                    attr = getattr(_raw, name)
                    if name in ("sync", "gpsimd", "vector", "scalar", "tensor"):
                        def wrapped(fn, *a, **kw):
                            n0 = nc.n_instructions()
                            r = attr(fn, *a, **kw)
                            _records.append((name, fn.__name__, n0,
                                             nc.n_instructions()))
                            return r
                        return wrapped
                    return attr
            block = _BlockProxy()
        N_ENG = 5
        phase_i = [0]

        def barrier():
            phase_i[0] += 1
            tgt = N_ENG * phase_i[0]

            def fs(eng):
                eng.wait_ge(sx.h, sx.n)
                for s_ in dma_sems:
                    eng.wait_ge(s_.h, s_.n)
                eng.nop().then_inc(phb.h, 1)
                eng.wait_ge(phb.h, tgt)

            def fg(eng):
                eng.wait_ge(sg.h, sg.n)
                eng.nop().then_inc(phb.h, 1)
                eng.wait_ge(phb.h, tgt)

            def f(eng):
                eng.nop().then_inc(phb.h, 1)
                eng.wait_ge(phb.h, tgt)
            block.sync(fs)
            block.gpsimd(fg)
            for m_ in (block.vector, block.scalar, block.tensor):
                m_(f)
            phb.n = tgt

        def branch_dma(g, par, sem, out0, out1, in0, in1=None):
            if solo:
                sem.inc(g.dma_start(out=out0, in_=in0), 16)
                sem.inc(g.dma_start(out=out1, in_=(in0 if in1 is None else in1)), 16)
                return
            with g.If_eq(par, 0):
                g.dma_start(out=out0, in_=in0).then_inc(sem.h, 16)
            with g.Else():
                g.dma_start(out=out1, in_=(in0 if in1 is None else in1))\
                    .then_inc(sem.h, 16)
            sem.n += 16

        with nc.psum_tensor("ps_boot", [P, 512], F32) as ps_boot:
            def ph0_sync(s):
                for dst, src in [(c_triu, triu_d), (c_identb, identb_d),
                                 (c_identf, identf_d), (c_uxh, uxh_d), (c_ones, ones_d),
                                 (c_mask, mask_d), (c_consts, consts_d),
                                 (c_lutrow, lut_d), (nonce_sb, nonce_d),
                                 (parity_sb, parity_d)]:
                    sx.inc(s.dma_start(out=dst[:], in_=src), 16)
            block.sync(ph0_sync)
            sx_ph0 = sx.n

            def ph0_gp(g):
                sg.inc(g.dma_start(out=c_wf[:].rearrange("p (dc h) -> p dc h", dc=DC),
                                   in_=wf_d.rearrange("(dc p) h -> p dc h", p=P)), 16)
            block.gpsimd(ph0_gp)

            def ph0_vec(v):
                v.wait_ge(sx.h, sx_ph0)
                sv(v); dv.inc(v.tensor_scalar(flagrow[:], nonce_sb[:].broadcast_to((1, P)),
                                              0, None, op0=ALU.add))
            block.vector(ph0_vec)

            def ph0_te(te):
                te.wait_ge(sx.h, sx_ph0)
                pe.inc(te.matmul(ps_boot[:, 0:16], c_ones[:], c_lutrow[:],
                                 start=True, stop=True))
            block.tensor(ph0_te)
            pe_ph0 = pe.n

            def ph0_vec2(v):
                v.wait_ge(pe.h, pe_ph0)
                sv(v); dv.inc(v.tensor_copy(c_lutcols[:], ps_boot[:, 0:16]))
            block.vector(ph0_vec2)
        barrier()
        if phases < 1:
            return nc

        DW = min(512, D)
        OCINS = c.OSG * max(1, (OCH // 2) // c.OSG)
        NTRH = DC * NTT
        STW = 2 if solo else 1

        fstk = ExitStack()
        fsb = lambda n_, s_, d_: fstk.enter_context(nc.sbuf_tensor(n_, s_, d_))
        mat_bf = [fsb(f"mat_bf{i}", [P, DC * P], BF16) for i in range(2)]
        mat_pl = fsb("mat_pl", [P, DC * P], BF16)
        mat_ac = [fsb(f"mat_ac{i}", [P, DC * P], BF16) for i in range(2)]

        dv1 = dv.n
        slda = [dmasem("slda0"), dmasem("slda1")]
        sldb = [dmasem("sldb0"), dmasem("sldb1")]
        sgsa = [dmasem("sgsa0"), dmasem("sgsa1")]
        sgsb = [dmasem("sgsb0"), dmasem("sgsb1")]

        ffn_slots = ([dmasem("slx0"), dmasem("slx1")],
                     [dmasem("slg0"), dmasem("slg1")],
                     [dmasem("sld0"), dmasem("sld1")],
                     dmasem("slr"), dmasem("slo"),
                     [dmasem("slxg0"), dmasem("slxg1")])

        SILU_MARGIN = 3
        script = []
        for grp in range(NFLG):
            for oc in range(grp * c.OSG, min((grp + 1) * c.OSG, OCH)):
                script.append(("mata", oc))
            if grp >= SILU_MARGIN and phases >= 2:
                g2 = grp - SILU_MARGIN
                for oc in range(g2 * c.OSG, min((g2 + 1) * c.OSG, OCH)):
                    script.append(("silu", oc))
        if phases >= 2:
            for grp in range(max(0, NFLG - SILU_MARGIN), NFLG):
                for oc in range(grp * c.OSG, min((grp + 1) * c.OSG, OCH)):
                    script.append(("silu", oc))
        for rb in range(NRBF):
            script.append(("matb", rb))
        dv_mata, dv_matb, dv_silu_pred = {}, {}, {}

        def walk_script(base):
            pos = base
            for kind, i in script:
                if kind == "silu":
                    pos += 1
                    dv_silu_pred[i] = pos
                else:
                    pos += 32
                    (dv_mata if kind == "mata" else dv_matb)[i] = pos
            return pos

        gq_base = gq.n

        def gq_gu(grp): return gq_base + grp + 1
        def gq_wdp(i): return gq_base + NFLG + i + 1
        gq_final = gq_base + NFLG + NRBF

        def make_ffn(tag, src, dst, first_pass, stack, ps_fn):
            F = {}
            b = lambda n_, s_, d_: stack.enter_context(nc.sbuf_tensor(n_, s_, d_))
            xb = [b(f"{tag}_x{i}", [P, D], F32) for i in range(2)]
            xsq = b(f"{tag}_xsq", [P, D], F32)
            stat = b(f"{tag}_stat", [P, 8], F32)
            xlt = [b(f"{tag}_xlt{i}", [P, DC * TF], BF16) for i in range(2)]
            xln_tok = b(f"{tag}_xlntok", [P, D], F32)
            h_sb = b(f"{tag}_h", [P, OCH * TF], BF16)
            NW = c.OSG * P
            wgp = [b(f"{tag}_wgp{i}", [P, DC * NW], BF16) for i in range(2)]
            wup = [b(f"{tag}_wup{i}", [P, DC * NW], BF16) for i in range(2)]
            sgl = b(f"{tag}_sgl", [P, TF], F32)
            wds = [b(f"{tag}_wds{i}", [P, D], BF16) for i in range(2)]
            xr = b(f"{tag}_xr", [P, D], F32)
            ot = b(f"{tag}_ot", [P, D], F32)

            (xslot, guslot, dnslot, xrslot, otslot, xgslot) = ffn_slots
            ps_g = ps_fn[0][:, 0:TF]
            ps_u = ps_fn[1][:, 0:TF]

            st = dict(free={},
                      xs={}, sq={}, srt={}, xln={},
                      tr={}, tr_done={}, evac_done={}, stats_done={},
                      sil_done={}, dvec_done={}, last_tr_pe=0,
                      gu_base={})
            F["st"] = st

            LNV = 7 + DC

            def tr_slot(dc_):
                return ps_fn[2 + (dc_ % 2)][:, 0:P]

            def ln_load(half, tts, fence=(), pred_dv=None, pred_ac=None, gp=False):
                slots_ = xgslot if gp else xslot

                def body(s):
                    for h_, v_ in fence:
                        s.wait_ge(h_, v_)
                    for tt in tts:
                        slot = (half * NTT + tt) % 2
                        gprev = half * NTT + tt - 2
                        if gprev >= 0:
                            key = (gprev // NTT, gprev % NTT)
                            if key in st["free"]:
                                dvf, acf = st["free"][key]
                                s.wait_ge(dv.h, dvf)
                                s.wait_ge(ac.h, acf)
                            else:
                                ptt = key[1]
                                s.wait_ge(dv.h, pred_dv + LNV * ptt + 7)
                                if key in st["sq"]:
                                    s.wait_ge(ac.h, st["sq"][key])
                                else:
                                    s.wait_ge(ac.h, pred_ac + 2 * ptt + 1)
                        slots_[slot].inc(s.dma_start(
                            out=xb[slot][:],
                            in_=src[half * TF + tt * P:half * TF + (tt + 1) * P, :]),
                            16)
                        rec = (slots_[slot].h, slots_[slot].n)
                        if (half, tt) in st["xs"]:
                            assert st["xs"][(half, tt)] == rec, (tag, half, tt)
                        st["xs"][(half, tt)] = rec
                if gp:
                    block.gpsimd(body)
                else:
                    block.sync(body)
            F["ln_load"] = ln_load

            def ln_promise(half):
                cnt = {0: 0, 1: 0}
                for tt in range(NTT):
                    if (half, tt) not in st["xs"]:
                        slot = (half * NTT + tt) % 2
                        sl = xgslot if tt >= 2 else xslot
                        cnt[slot] += 1
                        st["xs"][(half, tt)] = (sl[slot].h,
                                                sl[slot].n + 16 * cnt[slot])

            def ln_sq_op(s, half, tt, dv0p):
                s.wait_ge(*st["xs"][(half, tt)])
                if tt >= 1:
                    s.wait_ge(dv.h, dv0p + LNV * (tt - 1) + 3)
                elif half >= 1:
                    s.wait_ge(dv.h, st["stats_done"][half - 1])
                ss(s); ac.inc(s.activation(xsq[:], xb[(half * NTT + tt) % 2][:],
                                           AF.Square, accum_out=stat[:, 2:3]))
                st["sq"][(half, tt)] = ac.n

            def ln_srt_op(s, half, tt):
                dv0p = st[("dv0p", half)]
                s.wait_ge(dv.h, dv0p + LNV * tt + 5)
                if tt >= 1:
                    s.wait_ge(dv.h, dv0p + LNV * (tt - 1) + 6)
                ss(s); ac.inc(s.activation(stat[:, 4:5], stat[:, 3:4], AF.Sqrt,
                                           bias=ccol(COL_LNEPS)))
                st["srt"][(half, tt)] = ac.n

            def ln_acinline_ops(s, half):
                dv0p = st[("dv0p", half)]
                for tt in range(1, NTT):
                    ln_sq_op(s, half, tt, dv0p)
                    ln_srt_op(s, half, tt - 1)
                ln_srt_op(s, half, NTT - 1)

            def ln_sc(half, base=None):
                dv0p = dv.n if base is None else base
                st[("dv0p", half)] = dv0p
                ln_promise(half)

                def body(s):
                    for tt in range(NTT if base is None else 1):
                        ln_sq_op(s, half, tt, dv0p)
                        if base is None:
                            s.wait_ge(dv.h, dv0p + LNV * tt + 5)
                            if tt >= 1:
                                s.wait_ge(dv.h, dv0p + LNV * (tt - 1) + 6)
                            ss(s); ac.inc(s.activation(stat[:, 4:5], stat[:, 3:4],
                                                       AF.Sqrt,
                                                       bias=ccol(COL_LNEPS)))
                            st["srt"][(half, tt)] = ac.n
                block.scalar(body)
            F["ln_sc"] = ln_sc

            def ln_stats_ops(v, half, tr_pred):
                dv0p = st[("dv0p", half)]
                assert dv.n == dv0p, (tag, half, dv.n, dv0p)
                for tt in range(NTT):
                    slot = (half * NTT + tt) % 2
                    v.wait_ge(*st["xs"][(half, tt)])
                    sv(v); dv.inc(v.reduce_sum(stat[:, 0:1], xb[slot][:], axis=AX.X))
                    sv(v); dv.inc(v.tensor_scalar(stat[:, 1:2], stat[:, 0:1], 1.0 / D,
                                                  None, op0=ALU.mult))
                    v.wait_ge(ac.h, st["sq"][(half, tt)])
                    if tt >= 1:
                        v.wait_ge(ac.h, st["srt"][(half, tt - 1)])
                    sv(v); dv.inc(v.tensor_scalar(stat[:, 3:4], stat[:, 2:3], 1.0 / D,
                                                  None, op0=ALU.mult))
                    sv(v); dv.inc(v.tensor_tensor(stat[:, 5:6], stat[:, 1:2],
                                                  stat[:, 1:2], op=ALU.mult))
                    sv(v); dv.inc(v.tensor_tensor(stat[:, 3:4], stat[:, 3:4],
                                                  stat[:, 5:6], op=ALU.subtract))
                    v.wait_ge(ac.h, st["srt"][(half, tt)])
                    sv(v); dv.inc(v.reciprocal(stat[:, 6:7], stat[:, 4:5]))
                    if tt >= 1:
                        rec = st["tr"].get((half, tt - 1, DC - 1))
                        if rec is not None:
                            assert rec == tr_pred(tt - 1, DC - 1), (tag, half, tt)
                        v.wait_ge(pe.h, tr_pred(tt - 1, DC - 1))
                    elif st["last_tr_pe"]:
                        v.wait_ge(pe.h, st["last_tr_pe"])
                    sv(v); dv.inc(v.tensor_scalar(xln_tok[:], xb[slot][:], stat[:, 1:2],
                                                  stat[:, 6:7], op0=ALU.subtract,
                                                  op1=ALU.mult))
                    st["xln"][(half, tt)] = dv.n
                    st["free"][(half, tt)] = (dv.n, st["sq"][(half, tt)])
                    for dc_ in range(DC):
                        rec = st["tr"].get((half, tt, dc_))
                        if rec is not None:
                            assert rec == tr_pred(tt, dc_), (tag, half, tt, dc_)
                        v.wait_ge(pe.h, tr_pred(tt, dc_))
                        sv(v); dv.inc(v.tensor_copy(
                            xlt[half % 2][:, dc_ * TF + tt * P:dc_ * TF + (tt + 1) * P],
                            tr_slot(dc_)))
                        assert dv.n == dv0p + LNV * tt + 8 + dc_, (tag, half, tt, dc_)
                st["stats_done"][half] = dv.n
                st["evac_done"][half] = dv.n
                st["last_tr_pe"] = tr_pred(NTT - 1, DC - 1)

            def ln_stats(half, tr_pred):
                def body(v):
                    ln_stats_ops(v, half, tr_pred)
                block.vector(body)
            F["ln_stats"] = ln_stats

            def ln_tr_ops(te, half):
                dv0p = st[("dv0p", half)]
                for tt in range(NTT):
                    te.wait_ge(dv.h, dv0p + LNV * tt + 7)
                    for dc_ in range(DC):
                        if dc_ >= 2:
                            te.wait_ge(dv.h, dv0p + LNV * tt + 8 + (dc_ - 2))
                        elif tt >= 1:
                            te.wait_ge(dv.h, dv0p + LNV * (tt - 1) + 8 + DC - 2 + dc_)
                        pe.inc(te.transpose(tr_slot(dc_),
                                            xln_tok[:, dc_ * P:(dc_ + 1) * P],
                                            c_identf[:]))
                        st["tr"][(half, tt, dc_)] = pe.n
                st["tr_done"][half] = pe.n

            def ln_te(half, pre_dv=None):
                def body(te):
                    if pre_dv:
                        te.wait_ge(dv.h, pre_dv)
                    ln_tr_ops(te, half)
                block.tensor(body)
            F["ln_te"] = ln_te

            def trofs(half, oc):
                return NTRH if (st.get(("gu_tr", half)) and oc >= OCINS) else 0

            def pe_g(half, oc): return st[("peg0", half)] + 2 * oc + 1 + trofs(half, oc)
            def pe_u(half, oc): return st[("peg0", half)] + 2 * oc + 2 + trofs(half, oc)
            def ac_sil(half, oc):
                return st[("acg0", half)] + oc + 1 + \
                    (2 * NTT - 1 if (st.get(("gu_sqrt", half)) and oc >= OCINS)
                     else 0)

            NST = (7 + DC) * NTT

            def dv_h(half, oc):
                return st[("dvh0", half)] + oc + 1 + \
                    (NST if (st.get(("gu_tr", half)) and oc >= OCINS) else 0)
            F["pe_u"] = pe_u
            F["ac_sil"] = ac_sil

            def gu_done(half, grp):
                g0 = st["gu_base"][half]
                return (guslot[grp % 2].h, g0[grp % 2] + 32 * (grp // 2 + 1))

            def gu_sync(half, chase=False, has_tr=False):
                st["gu_base"][half] = [guslot[0].n, guslot[1].n]
                st[("peg0", half)] = pe.n
                st[("dvh0", half)] = dv.n
                st[("gu_tr", half)] = has_tr

                def body(s):
                    for grp in range(NFLG):
                        if chase:
                            s.wait_ge(gq.h, gq_gu(grp))
                        elif first_pass and grp == 0:
                            s.wait_ge(gq.h, gq_base + NFLG)
                        o0 = grp * NW
                        o1 = min(o0 + NW, INTER)
                        if grp >= 2:
                            lastoc = min((grp - 1) * c.OSG, OCH) - 1
                            s.wait_ge(pe.h, pe_u(half, lastoc))
                        elif grp == 0:
                            s.wait_ge(pe.h, st[("peg0", half)])
                        guslot[grp % 2].inc(s.dma_start(
                            out=wgp[grp % 2][:, 0:DC * (o1 - o0)]
                                .rearrange("p (dc o) -> p dc o", dc=DC),
                            in_=wgt_s[:, o0:o1]
                                .rearrange("(dc p) o -> p dc o", p=P)), 16)
                        guslot[grp % 2].inc(s.dma_start(
                            out=wup[grp % 2][:, 0:DC * (o1 - o0)]
                                .rearrange("p (dc o) -> p dc o", dc=DC),
                            in_=wut_s[:, o0:o1]
                                .rearrange("(dc p) o -> p dc o", p=P)), 16)
                block.sync(body)
            F["gu_sync"] = gu_sync

            def gu_main(half, nxt=None, chase=False, dvh_fn=None):
                assert st[("gu_tr", half)] == (nxt is not None)
                st[("acg0", half)] = ac.n
                st[("gu_sqrt", half)] = nxt is not None
                if dvh_fn is None:
                    dvh_fn = lambda oc: dv_h(half, oc)
                assert pe.n == st[("peg0", half)], (tag, half)
                if not chase:
                    assert dv.n == st[("dvh0", half)], (tag, half, dv.n)
                prev_done = st["dvec_done"].get(half - 1)

                def body(te):
                    if prev_done:
                        te.wait_ge(dv.h, prev_done)
                    te.wait_ge(dv.h, st["evac_done"][half])
                    inserted = [False]
                    for grp in range(NFLG):
                        o0 = grp * NW
                        ow = min((grp + 1) * NW, INTER) - o0
                        if chase:
                            te.wait_ge(gq.h, gq_gu(grp))
                        te.wait_ge(*gu_done(half, grp))
                        for ocw in range(ow // P):
                            oc = grp * c.OSG + ocw
                            if nxt is not None and not inserted[0] and oc >= OCINS:
                                assert pe.n == st[("peg0", half)] + 2 * OCINS
                                ln_tr_ops(te, nxt)
                                inserted[0] = True
                            if oc >= 1:
                                te.wait_ge(ac.h, ac_sil(half, oc - 1))
                            for n2 in range(GW):
                                for dc_ in range(DC):
                                    mm = te.matmul(
                                        ps_g[:, n2 * GWW:(n2 + 1) * GWW],
                                        wgp[grp % 2][:, dc_ * ow + ocw * P:
                                                     dc_ * ow + (ocw + 1) * P],
                                        xlt[half % 2][:, dc_ * TF + n2 * GWW:
                                                      dc_ * TF + (n2 + 1) * GWW],
                                        start=(dc_ == 0), stop=(dc_ == DC - 1))
                            pe.inc(mm)
                            assert pe.n == pe_g(half, oc)
                            if oc >= 1:
                                te.wait_ge(dv.h, dvh_fn(oc - 1))
                            for n2 in range(GW):
                                for dc_ in range(DC):
                                    mm = te.matmul(
                                        ps_u[:, n2 * GWW:(n2 + 1) * GWW],
                                        wup[grp % 2][:, dc_ * ow + ocw * P:
                                                     dc_ * ow + (ocw + 1) * P],
                                        xlt[half % 2][:, dc_ * TF + n2 * GWW:
                                                      dc_ * TF + (n2 + 1) * GWW],
                                        start=(dc_ == 0), stop=(dc_ == DC - 1))
                            pe.inc(mm)
                    if nxt is not None and not inserted[0]:
                        ln_tr_ops(te, nxt)
                block.tensor(body)
            F["gu_main"] = gu_main

            def gu_sc(half, dvh_fn=None, sqrt_half=None):
                assert st[("gu_sqrt", half)] == (sqrt_half is not None), (tag, half)
                if dvh_fn is None:
                    dvh_fn = lambda oc: dv_h(half, oc)
                assert ac.n == st[("acg0", half)], (tag, half)

                def body(s):
                    for oc in range(OCH):
                        if sqrt_half is not None and oc == OCINS:
                            ln_acinline_ops(s, sqrt_half)
                        s.wait_ge(pe.h, pe_g(half, oc))
                        if oc >= 1:
                            s.wait_ge(dv.h, dvh_fn(oc - 1))
                        ss(s); ac.inc(s.activation(sgl[:], ps_g[:], AF.Silu))
                        assert ac.n == ac_sil(half, oc)
                block.scalar(body)
            F["gu_sc"] = gu_sc

            def silu_op(v, half, oc):
                v.wait_ge(ac.h, ac_sil(half, oc))
                v.wait_ge(pe.h, pe_u(half, oc))
                sv(v); dv.inc(v.tensor_tensor(h_sb[:, oc * TF:(oc + 1) * TF],
                                              sgl[:], ps_u[:], op=ALU.mult))
            F["silu_op"] = silu_op

            def gu_vec(half, nxt=None):
                assert dv.n == st[("dvh0", half)], (tag, half, dv.n)

                def body(v):
                    for oc in range(OCINS if nxt is not None else OCH):
                        silu_op(v, half, oc)
                        assert dv.n == dv_h(half, oc)
                    if nxt is not None:
                        peg0 = st[("peg0", half)]
                        ln_stats_ops(v, nxt, lambda tt, dc_:
                                     peg0 + 2 * OCINS + tt * DC + dc_ + 1)
                        for oc in range(OCINS, OCH):
                            silu_op(v, half, oc)
                            assert dv.n == dv_h(half, oc)
                block.vector(body)
                st["sil_done"][half] = dv.n
            F["gu_vec"] = gu_vec

            def ocseq_of(chase):
                if not chase:
                    return list(range(OCH))
                seq = []
                for i in range(NRBF):
                    if i < OCH:
                        seq.append(i)
                    if NRBF + i < OCH:
                        seq.append(NRBF + i)
                return seq

            def dn_sync(half, chase=False):
                dns0 = [dnslot[0].n, dnslot[1].n]
                xrs0 = xrslot.n
                ots0 = otslot.n
                ped0 = pe.n
                dvd0 = dv.n
                st[("dn", half)] = (dns0, xrs0, ots0, ped0, dvd0)
                ocseq = ocseq_of(chase)
                NE = NTG * NGD

                prev_evac = st["dvec_done"].get(half - 1, 0)

                def body(s):
                    if first_pass and not chase:
                        s.wait_ge(gq.h, gq_final)

                    def xr_load(e):
                        if e == 0:
                            if prev_evac:
                                s.wait_ge(dv.h, prev_evac)
                        else:
                            s.wait_ge(dv.h, dvd0 + e * NDS)
                        tg_e, j_e = e // NGD, e % NGD
                        xrslot.inc(s.dma_start(
                            out=xr[:],
                            in_=src[half * TF + tg_e * TG + j_e * P:
                                    half * TF + tg_e * TG + (j_e + 1) * P, :]), 16)

                    xr_load(0)
                    nxt_e = 1
                    for tg_ in range(NTG):
                        for seq, oc in enumerate(ocseq):
                            k = tg_ * OCH + seq
                            if chase:
                                s.wait_ge(gq.h,
                                          gq_wdp(oc if oc < NRBF else oc - NRBF))
                            s.wait_ge(pe.h, ped0 if k < 2 else ped0 + k - 1)
                            dnslot[k % 2].inc(s.dma_start(
                                out=wds[k % 2][:],
                                in_=wdt_s[oc * P:(oc + 1) * P, :]), 16)
                        while nxt_e <= min((tg_ + 1) * NGD, NE - 1):
                            xr_load(nxt_e)
                            nxt_e += 1
                block.sync(body)
            F["dn_sync"] = dn_sync

            def dn_te(half, chase=False, dvh_last=None, evac_next=None):
                (dns0, xrs0, ots0, ped0, dvd0) = st[("dn", half)]
                assert pe.n == ped0, (tag, half)
                if dvh_last is None:
                    dvh_last = st["sil_done"][half]
                ocseq = ocseq_of(chase)

                def body(te):
                    te.wait_ge(dv.h, dvh_last)
                    if evac_next:
                        te.wait_ge(dv.h, evac_next)
                    for tg_ in range(NTG):
                        if tg_ >= 1:
                            te.wait_ge(dv.h, dvd0 + tg_ * NGD * NDS)
                        for seq, oc in enumerate(ocseq):
                            k = tg_ * OCH + seq
                            if chase:
                                te.wait_ge(gq.h,
                                           gq_wdp(oc if oc < NRBF else oc - NRBF))
                            te.wait_ge(dnslot[k % 2].h,
                                       dns0[k % 2] + 16 * (k // 2 + 1))
                            for j in range(NGD):
                                for ds_ in range(NDS):
                                    mm = te.matmul(
                                        ps_fn[j * NDS + ds_][:, 0:DW],
                                        h_sb[:, oc * TF + tg_ * TG + j * P:
                                             oc * TF + tg_ * TG + (j + 1) * P],
                                        wds[k % 2][:, ds_ * DW:(ds_ + 1) * DW],
                                        start=(seq == 0), stop=(seq == OCH - 1))
                            pe.inc(mm)
                block.tensor(body)
            F["dn_te"] = dn_te

            def dn_vec(half):
                (dns0, xrs0, ots0, ped0, dvd0) = st[("dn", half)]
                assert dv.n == dvd0, (tag, half, dv.n)

                def body(v):
                    for tg_ in range(NTG):
                        v.wait_ge(pe.h, ped0 + (tg_ + 1) * OCH)
                        for j in range(NGD):
                            e = tg_ * NGD + j
                            v.wait_ge(xrslot.h, xrs0 + 16 * (e + 1))
                            v.wait_ge(otslot.h, ots0 + 16 * e)
                            for ds_ in range(NDS):
                                sv(v); dv.inc(v.scalar_tensor_tensor(
                                    ot[:, ds_ * DW:(ds_ + 1) * DW],
                                    ps_fn[j * NDS + ds_][:, 0:DW], 0.5,
                                    xr[:, ds_ * DW:(ds_ + 1) * DW],
                                    op0=ALU.mult, op1=ALU.add))
                block.vector(body)
                st["dvec_done"][half] = dv.n
                assert dv.n == dvd0 + NTG * NGD * NDS
            F["dn_vec"] = dn_vec

            def dn_store(half):
                (dns0, xrs0, ots0, ped0, dvd0) = st[("dn", half)]

                def body(g):
                    for tg_ in range(NTG):
                        for j in range(NGD):
                            e = tg_ * NGD + j
                            g.wait_ge(dv.h, dvd0 + (e + 1) * NDS)
                            otslot.inc(g.dma_start(
                                out=dst[half * TF + tg_ * TG + j * P:
                                        half * TF + tg_ * TG + (j + 1) * P, :],
                                in_=ot[:]), 16)
                    g.wait_ge(otslot.h, otslot.n)
                block.gpsimd(body)
            F["dn_store"] = dn_store

            def template(half):
                import os as _os
                nxt = half + 1 if half + 1 < c.NHALF else None
                if _os.environ.get("TPLNOINL"):
                    nxt = None
                gu_sync(half, has_tr=(nxt is not None))
                if nxt is not None:
                    ln_sc(nxt, base=st[("dvh0", half)] + OCINS)
                gu_main(half, nxt=nxt)
                gu_sc(half, sqrt_half=nxt)
                if nxt is not None and NTT > 2:
                    ln_load(nxt, range(2, NTT), pred_dv=st[("dv0p", nxt)], gp=True)
                if nxt is not None and half + 2 < c.NHALF:
                    ln_load(half + 2, range(min(2, NTT)),
                            pred_dv=st[("dv0p", half + 1)])
                gu_vec(half, nxt=nxt)
                dn_sync(half)
                dn_te(half, evac_next=(st["evac_done"].get(nxt)
                                       if nxt is not None else None))
                dn_vec(half)
                dn_store(half)
            F["template"] = template

            def prologue(fence=()):
                ln_load(0, range(NTT), fence=fence, pred_dv=dv.n, pred_ac=ac.n)
                ln_sc(0)
                pebase = pe.n
                ln_stats(0, lambda tt, dc_, pb=pebase: pb + tt * DC + dc_ + 1)
                ln_te(0)
                if c.NHALF > 1:
                    ln_load(1, range(min(2, NTT)))
            F["prologue"] = prologue
            return F

        ps_f1 = [fstk.enter_context(nc.psum_tensor(f"f1_ps{i}", [P, 512], F32))
                 for i in range(max(4, NGD * NDS))]
        ffn1 = make_ffn("f1", x_in, x1_park, True, fstk, ps_f1)

        ffn1["prologue"](fence=[(phb.h, N_ENG * phase_i[0])])
        dv_mat_end = walk_script(dv.n)

        def slda_done(oc): return (slda[oc % 2].h, 16 * (oc // 2 + 1))
        def sldb_done(rb): return (sldb[rb % 2].h, 16 * (rb // 2 + 1))

        def ph1_gp(g):
            def load_a(oc):
                if oc >= 2:
                    g.wait_ge(dv.h, dv_mata[oc - 2])
                slda[oc % 2].inc(g.dma_start(
                    out=mat_bf[oc % 2][:].rearrange("p (a o) -> p a o", a=DC),
                    in_=mata_d[oc * D:(oc + 1) * D, :]
                        .rearrange("(a p) o -> p a o", p=P)), 16)

            def load_b(rb):
                if rb >= 2:
                    g.wait_ge(dv.h, dv_matb[rb - 2])
                else:
                    g.wait_ge(dv.h, dv_mata[OCH - 1])
                sldb[rb % 2].inc(g.dma_start(out=mat_bf[rb % 2][:],
                                             in_=matb_d[rb * P:(rb + 1) * P, :]), 16)

            with g.register("par") as par, g.register("want") as want, \
                 g.register("seen") as seen, g.register("more") as more, \
                 g.register("tgt") as tgt:
                g.load(par, parity_sb[0:1, 0:1])
                g.load(want, nonce_sb[0:1, 0:1])
                g.reg_mov(tgt, 0)

                def poll_row(row):
                    g.reg_mov(more, 1)
                    with g.While(more):
                        g.reg_add(tgt, tgt, 16)
                        g.dma_start(out=flagrd[:],
                                    in_=flags_s[row:row + 1, :]).then_inc(spoll.h, 16)
                        g.wait_ge(spoll.h, tgt)
                        g.load(seen, flagrd[0:1, 0:1])
                        g.reg_alu(more, seen, want, ALU.not_equal)

                load_a(0)
                if OCH > 1:
                    load_a(1)
                for grp in range(NFLG):
                    for oc in range(grp * c.OSG, min((grp + 1) * c.OSG, OCH)):
                        if oc + 2 < OCH:
                            load_a(oc + 2)
                        g.wait_ge(dv.h, dv_mata[oc])
                        branch_dma(g, par, sgsa[oc % 2],
                                   wgt_s[:, oc * P:(oc + 1) * P]
                                   .rearrange("(a p) o -> p a o", p=P),
                                   wut_s[:, oc * P:(oc + 1) * P]
                                   .rearrange("(a p) o -> p a o", p=P),
                                   mat_ac[oc % 2][:].rearrange("p (a o) -> p a o",
                                                               a=DC))
                    g.wait_ge(sgsa[0].h, sgsa[0].n)
                    g.wait_ge(sgsa[1].h, sgsa[1].n)
                    branch_dma(g, par, sg, flags_s[grp:grp + 1, :],
                               flags_s[NFLG + grp:NFLG + grp + 1, :], flagrow[:])
                    g.wait_ge(sg.h, sg.n)
                    if grp >= 1:
                        poll_row(grp - 1)
                        poll_row(NFLG + grp - 1)
                        gq.inc(g.nop())
                poll_row(NFLG - 1)
                poll_row(2 * NFLG - 1)
                gq.inc(g.nop())
                load_b(0)
                if NRBF > 1:
                    load_b(1)
                for rb in range(NRBF):
                    if rb + 2 < NRBF:
                        load_b(rb + 2)
                    g.wait_ge(dv.h, dv_matb[rb])
                    branch_dma(g, par, sgsb[rb % 2],
                               wdt_s[rb * P:(rb + 1) * P, :],
                               wdt_s[IHE + rb * P:IHE + (rb + 1) * P, :],
                               mat_ac[rb % 2][:])
                    g.wait_ge(sgsb[rb % 2].h, sgsb[rb % 2].n)
                    branch_dma(g, par, sg, flags_s[WDE0 + rb:WDE0 + rb + 1, :],
                               flags_s[WDO0 + rb:WDO0 + rb + 1, :], flagrow[:])
                    g.wait_ge(sg.h, sg.n)
                    if rb >= 1:
                        poll_row(WDE0 + rb - 1)
                        poll_row(WDO0 + rb - 1)
                        gq.inc(g.nop())
                poll_row(WDE0 + NRBF - 1)
                poll_row(WDO0 + NRBF - 1)
                gq.inc(g.nop())
        block.gpsimd(ph1_gp)
        assert gq.n == gq_final, (gq.n, gq_final)

        ffn1["st"][("peg0", 0)] = pe.n
        ffn1["st"][("acg0", 0)] = ac.n
        ffn1["st"][("gu_tr", 0)] = False
        dv_silu = {}

        def ph1_vec(v):
            NEV = (OCH + 1) // 2
            NOD = OCH // 2
            for kind, idx in script:
                if kind == "silu":
                    ffn1["silu_op"](v, 0, idx)
                    dv_silu[idx] = dv.n
                    assert dv.n == dv_silu_pred[idx], (idx, dv.n)
                    continue
                if kind == "mata":
                    oc = idx
                    v.wait_ge(*slda_done(oc))
                    v.wait_ge(sgsa[oc % 2].h, 16 * STW * (oc // 2))
                    buf, tgtpos = mat_bf[oc % 2][:], dv_mata[oc]
                    acc = mat_ac[oc % 2][:]
                else:
                    rb = idx
                    v.wait_ge(*sldb_done(rb))
                    if rb < 2:
                        v.wait_ge(sgsa[rb % 2].h, 16 * STW * (NEV if rb == 0 else NOD))
                    else:
                        v.wait_ge(sgsb[rb % 2].h, 16 * STW * (rb // 2))
                    buf, tgtpos = mat_bf[rb % 2][:], dv_matb[rb]
                    acc = mat_ac[rb % 2][:]
                for t in range(16):
                    sv(v); dv.inc(v.tensor_scalar(mat_pl[:], buf, float(t),
                                                  c_lutcols[:, t:t + 1],
                                                  op0=ALU.is_equal, op1=ALU.mult))
                    sv(v)
                    if t == 0:
                        dv.inc(v.tensor_copy(acc, mat_pl[:]))
                    else:
                        dv.inc(v.tensor_tensor(acc, acc, mat_pl[:], op=ALU.add))
                assert dv.n == tgtpos, (kind, idx, dv.n, tgtpos)
        block.vector(ph1_vec)
        assert dv.n == dv_mat_end

        if phases < 2:
            fstk.close()
            barrier()
            return nc

        ffn1["gu_sync"](0, chase=True, has_tr=False)
        ffn1["gu_main"](0, nxt=None, chase=True, dvh_fn=lambda oc: dv_silu[oc])
        ffn1["gu_sc"](0, dvh_fn=lambda oc: dv_silu[oc])
        ffn1["st"]["sil_done"][0] = dv_silu[OCH - 1]
        if ffn_parts < 2:
            fstk.close()
            barrier()
            return nc

        ffn1["dn_sync"](0, chase=True)
        ffn1["dn_te"](0, chase=True, dvh_last=dv_silu[OCH - 1])
        ffn1["dn_vec"](0)
        ffn1["dn_store"](0)
        if c.NHALF > 1:
            ffn1["ln_sc"](1)
            pebase = pe.n
            ffn1["ln_stats"](1, lambda tt, dc_, pb=pebase: pb + tt * DC + dc_ + 1)
            if NTT > 2:
                ffn1["ln_load"](1, range(2, NTT), gp=True)
            if c.NHALF > 2:
                ffn1["ln_load"](2, range(min(2, NTT)))
            ffn1["ln_te"](1, pre_dv=ffn1["st"]["dvec_done"][0])

        if ffn_parts >= 3:
            hi = c.NHALF if ffn_parts >= 5 else min(c.NHALF, ffn_parts - 1)
            for half in range(1, hi):
                ffn1["template"](half)
        fstk.close()
        barrier()
        if phases < 3:
            return nc

        attn_stack = ExitStack()
        atsb = lambda n_, s_, d_: attn_stack.enter_context(nc.sbuf_tensor(n_, s_, d_))
        S_sb = atsb("S_sb", [P, D], F32)
        S_bf = atsb("S_bf", [P, D], BF16)
        SR_bf = atsb("SR_bf", [P, D], BF16)
        ld_all = atsb("ld_all", [P, NT * NH], F32)
        expcum_all = atsb("expcum_all", [P, NT * NH], F32)
        Rtab = atsb("Rtab", [1, (NT + 1) * NH], F32)

        def emit_ln(tag, n_tiles, src_rows, xlt, xstride, xb, xsq, stat, xln_tok,
                    ps_tr, xslot, pre_pe_wait=None, fence=()):
            sx0, ac0, dv0, pe0 = sx.n, ac.n, dv.n, pe.n
            xs0 = [xslot[0].n, xslot[1].n]

            def xs_done(tt):
                return (xslot[tt % 2].h, xs0[tt % 2] + 16 * (tt // 2 + 1))
            LNV = 7 + DC

            def dv_tile(tt): return dv0 + LNV * (tt + 1)
            def dv_xln(tt): return dv0 + LNV * tt + 7
            def dv_var(tt): return dv0 + LNV * tt + 5
            def dv_evac(tt, dc_): return dv0 + LNV * tt + 8 + dc_
            def ac_sq(tt): return ac0 + 2 * tt + 1
            def ac_srt(tt): return ac0 + 2 * tt + 2
            def pe_tr(tt, dc_): return pe0 + DC * tt + dc_ + 1

            def f_sync(s):
                for h_, v_ in fence:
                    s.wait_ge(h_, v_)
                for tt in range(n_tiles):
                    if tt >= 2:
                        s.wait_ge(dv.h, dv_tile(tt - 2))
                        s.wait_ge(ac.h, ac_srt(tt - 2))
                    else:
                        s.wait_ge(dv.h, dv0)
                        s.wait_ge(ac.h, ac0)
                    xslot[tt % 2].inc(s.dma_start(out=xb[tt % 2][:], in_=src_rows(tt)), 16)
            block.sync(f_sync)

            def f_sc(s):
                for h_, v_ in fence:
                    s.wait_ge(h_, v_)
                for tt in range(n_tiles):
                    s.wait_ge(*xs_done(tt))
                    if tt >= 1:
                        s.wait_ge(dv.h, dv_var(tt - 1))
                    ss(s); ac.inc(s.activation(xsq[tt % 2][:], xb[tt % 2][:], AF.Square,
                                               accum_out=stat[:, 2:3]))
                    s.wait_ge(dv.h, dv_var(tt))
                    ss(s); ac.inc(s.activation(stat[:, 4:5], stat[:, 3:4], AF.Sqrt,
                                               bias=ccol(COL_LNEPS)))
            block.scalar(f_sc)

            def f_te(te):
                if pre_pe_wait is not None:
                    te.wait_ge(pre_pe_wait[0], pre_pe_wait[1])
                for tt in range(n_tiles):
                    te.wait_ge(dv.h, dv_xln(tt))
                    for dc_ in range(DC):
                        if dc_ >= 2:
                            te.wait_ge(dv.h, dv_evac(tt, dc_ - 2))
                        sl_ = (dc_ % 2) * 1024
                        pe.inc(te.transpose(ps_tr[:, sl_:sl_ + P],
                                            xln_tok[:, dc_ * P:(dc_ + 1) * P],
                                            c_identb[:]))
            block.tensor(f_te)

            def f_vec(v):
                for h_, v_ in fence:
                    v.wait_ge(h_, v_)
                for tt in range(n_tiles):
                    v.wait_ge(*xs_done(tt))
                    sv(v); dv.inc(v.reduce_sum(stat[:, 0:1], xb[tt % 2][:], axis=AX.X))
                    sv(v); dv.inc(v.tensor_scalar(stat[:, 1:2], stat[:, 0:1], 1.0 / D,
                                                  None, op0=ALU.mult))
                    v.wait_ge(ac.h, ac_sq(tt))
                    sv(v); dv.inc(v.tensor_scalar(stat[:, 3:4], stat[:, 2:3], 1.0 / D,
                                                  None, op0=ALU.mult))
                    sv(v); dv.inc(v.tensor_tensor(stat[:, 5:6], stat[:, 1:2],
                                                  stat[:, 1:2], op=ALU.mult))
                    sv(v); dv.inc(v.tensor_tensor(stat[:, 3:4], stat[:, 3:4],
                                                  stat[:, 5:6], op=ALU.subtract))
                    v.wait_ge(ac.h, ac_srt(tt))
                    sv(v); dv.inc(v.reciprocal(stat[:, 6:7], stat[:, 4:5]))
                    if tt >= 1:
                        v.wait_ge(pe.h, pe_tr(tt - 1, DC - 1))
                    sv(v); dv.inc(v.tensor_scalar(xln_tok[:], xb[tt % 2][:], stat[:, 1:2],
                                                  stat[:, 6:7], op0=ALU.subtract,
                                                  op1=ALU.mult))
                    for dc_ in range(DC):
                        v.wait_ge(pe.h, pe_tr(tt, dc_))
                        sl_ = (dc_ % 2) * 1024
                        sv(v); dv.inc(v.tensor_copy(
                            xlt[:, dc_ * xstride + tt * P:dc_ * xstride + (tt + 1) * P],
                            ps_tr[:, sl_:sl_ + P]))
            block.vector(f_vec)
            assert dv.n == dv0 + LNV * n_tiles, (dv.n, dv0, n_tiles)
            assert ac.n == ac0 + 2 * n_tiles
            assert pe.n == pe0 + DC * n_tiles
            return dict(dv_done=dv.n, dv_tile=dv_tile)

        with ExitStack() as ps3:
            asb = lambda n_, s_, d_: ps3.enter_context(nc.sbuf_tensor(n_, s_, d_))
            xb3 = [asb(f"a_x{i}", [P, D], F32) for i in range(2)]
            xsq3a = asb("a_xsq", [P, D], F32)
            xsq3 = [xsq3a, xsq3a]
            stat3 = asb("a_stat", [P, 8], F32)
            xlt3 = asb("a_xlt", [P, DC * TC], BF16)
            xln_tok3 = asb("a_xlntok", [P, D], BF16)
            wbig = asb("a_wbig", [P, DC * D], BF16)
            qev = asb("a_qev", [P, QW], BF16)
            kev = asb("a_kev", [P, D], BF16)
            zld = asb("a_zld", [P, 4 * NH], F32)

            with nc.psum_tensor("a_pstr", [P, 2048], BF16) as ps_tr3:
                ln3 = emit_ln("aln", NT,
                              lambda tt: x1_park[tt * P:(tt + 1) * P, :],
                              xlt3, TC, xb3, xsq3, stat3, xln_tok3, ps_tr3,
                              ffn_slots[0])
            dv_ln3 = ln3["dv_done"]

            W5 = min(512, D)
            PW = max(QW, W5, NH)
            with nc.psum_tensor("a_psq", [P, 2 * PW], F32) as ps_q:
                sg_q0 = sg.n

                def q_gp(g):
                    g.wait_ge(dv.h, dv_ln3)
                    sg.inc(g.dma_start(
                        out=wbig[:].rearrange("p (dc e) -> p dc e", dc=DC),
                        in_=wq_d.rearrange("(dc p) e -> p dc e", p=P)), 16)
                block.gpsimd(q_gp)
                peq0, acq0, sxq0 = pe.n, ac.n, sx.n
                NQ = DC * NQG

                def pe_q(k): return peq0 + k + 1
                def ac_q(k): return acq0 + k + 1
                def sx_q(k): return sxq0 + 16 * (QW // P) * (k + 1)

                def q_te(te):
                    te.wait_ge(sg.h, sg_q0 + 16)
                    te.wait_ge(dv.h, dv_ln3)
                    for ec in range(DC):
                        for n4 in range(NQG):
                            k = ec * NQG + n4
                            bank = k % 2
                            if k >= 2:
                                te.wait_ge(ac.h, ac_q(k - 2))
                            for dc_ in range(DC):
                                mm = te.matmul(
                                    ps_q[:, bank * PW:bank * PW + QW],
                                    wbig[:, dc_ * D + ec * P:dc_ * D + (ec + 1) * P],
                                    xlt3[:, dc_ * TC + n4 * QW:dc_ * TC + (n4 + 1) * QW],
                                    start=(dc_ == 0), stop=(dc_ == DC - 1))
                            pe.inc(mm)
                block.tensor(q_te)

                def q_sc(s):
                    for k in range(NQ):
                        s.wait_ge(pe.h, pe_q(k))
                        if k >= 1:
                            s.wait_ge(sx.h, sx_q(k - 1))
                        else:
                            s.wait_ge(sx.h, sxq0)
                        bank = k % 2
                        ss(s); ac.inc(s.activation(qev[:], ps_q[:, bank * PW:bank * PW + QW],
                                                   AF.Silu))
                block.scalar(q_sc)

                def q_store(s):
                    for ec in range(DC):
                        for n4 in range(NQG):
                            k = ec * NQG + n4
                            s.wait_ge(ac.h, ac_q(k))
                            for j in range(QW // P):
                                t_idx = n4 * (QW // P) + j
                                sx.inc(s.dma_start(
                                    out=q_park[t_idx * D + ec * P:t_idx * D + (ec + 1) * P, :],
                                    in_=qev[:, j * P:(j + 1) * P]), 16)
                block.sync(q_store)
                assert pe.n == peq0 + NQ and ac.n == acq0 + NQ
                assert sx.n == sxq0 + 16 * (QW // P) * NQ

                def kv_phase(wsrc, park, use_silu):
                    sgk0 = sg.n
                    pe_before = pe.n

                    def kv_gp(g):
                        g.wait_ge(pe.h, pe_before)
                        sg.inc(g.dma_start(
                            out=wbig[:].rearrange("p (dc e) -> p dc e", dc=DC),
                            in_=wsrc.rearrange("(dc p) e -> p dc e", p=P)), 16)
                    block.gpsimd(kv_gp)
                    pek0, ack0, sxk0 = pe.n, ac.n, sx.n

                    def pe_kv(k): return pek0 + k + 1
                    def ac_kv(k): return ack0 + k + 1
                    def sx_kv(tc_): return sxk0 + 16 * (tc_ + 1)

                    def kv_te(te):
                        te.wait_ge(sg.h, sgk0 + 16)
                        te.wait_ge(ac.h, ack0)
                        for tc_ in range(NT):
                            for es in range(NES):
                                k = tc_ * NES + es
                                bank = k % 2
                                if k >= 2:
                                    te.wait_ge(ac.h, ac_kv(k - 2))
                                for dc_ in range(DC):
                                    mm = te.matmul(
                                        ps_q[:, bank * PW:bank * PW + W5],
                                        xlt3[:, dc_ * TC + tc_ * P:dc_ * TC + (tc_ + 1) * P],
                                        wbig[:, dc_ * D + es * W5:dc_ * D + (es + 1) * W5],
                                        start=(dc_ == 0), stop=(dc_ == DC - 1))
                                pe.inc(mm)
                    block.tensor(kv_te)

                    def kv_sc(s):
                        for tc_ in range(NT):
                            for es in range(NES):
                                k = tc_ * NES + es
                                s.wait_ge(pe.h, pe_kv(k))
                                if tc_ >= 1 and es == 0:
                                    s.wait_ge(sx.h, sx_kv(tc_ - 1))
                                elif tc_ == 0 and es == 0:
                                    s.wait_ge(sx.h, sxk0)
                                bank = k % 2
                                fn = AF.Silu if use_silu else AF.Copy
                                ss(s); ac.inc(s.activation(
                                    kev[:, es * W5:(es + 1) * W5],
                                    ps_q[:, bank * PW:bank * PW + W5], fn))
                    block.scalar(kv_sc)

                    def kv_store(s):
                        for tc_ in range(NT):
                            s.wait_ge(ac.h, ac_kv(tc_ * NES + NES - 1))
                            sx.inc(s.dma_start(out=park[tc_ * P:(tc_ + 1) * P, :],
                                               in_=kev[:]), 16)
                    block.sync(kv_store)
                    assert pe.n == pek0 + NT * NES and ac.n == ack0 + NT * NES
                    assert sx.n == sxk0 + 16 * NT

                kv_phase(wk_d, k_park, True)
                kv_phase(wv_d, v_park, False)

                pel0, acl0, dvl0 = pe.n, ac.n, dv.n

                def pe_ld(tc_): return pel0 + tc_ + 1
                def ac_ld(tc_): return acl0 + 4 * (tc_ + 1)
                def dv_ld(tc_): return dvl0 + 2 * (tc_ + 1)

                def ld_te(te):
                    te.wait_ge(ac.h, acl0)
                    for tc_ in range(NT):
                        bank = tc_ % 2
                        if tc_ >= 2:
                            te.wait_ge(dv.h, dv_ld(tc_ - 2))
                            te.wait_ge(ac.h, ac_ld(tc_ - 2))
                        for dc_ in range(DC):
                            mm = te.matmul(
                                ps_q[:, bank * PW:bank * PW + NH],
                                xlt3[:, dc_ * TC + tc_ * P:dc_ * TC + (tc_ + 1) * P],
                                c_wf[:, dc_ * NH:(dc_ + 1) * NH],
                                start=(dc_ == 0), stop=(dc_ == DC - 1))
                        pe.inc(mm)
                block.tensor(ld_te)

                def ld_sc(s):
                    for tc_ in range(NT):
                        bank = tc_ % 2
                        s.wait_ge(pe.h, pe_ld(tc_))
                        if tc_ >= 1:
                            s.wait_ge(dv.h, dv_ld(tc_ - 1))
                        psl = ps_q[:, bank * PW:bank * PW + NH]
                        ss(s); ac.inc(s.activation(zld[:, 0:NH], psl, AF.Copy))
                        ss(s); ac.inc(s.activation(zld[:, NH:2 * NH], psl, AF.Abs))
                        ss(s); ac.inc(s.activation(zld[:, 2 * NH:3 * NH],
                                                   zld[:, NH:2 * NH], AF.Exp,
                                                   scale=ccol(COL_NEG1)))
                        ss(s); ac.inc(s.activation(zld[:, 3 * NH:4 * NH],
                                                   zld[:, 2 * NH:3 * NH], AF.Ln,
                                                   bias=ccol(COL_ONE)))
                block.scalar(ld_sc)

                def ld_vec(v):
                    for tc_ in range(NT):
                        v.wait_ge(ac.h, ac_ld(tc_))
                        sv(v); dv.inc(v.tensor_scalar_min(zld[:, 0:NH], zld[:, 0:NH], 0.0))
                        sv(v); dv.inc(v.tensor_tensor(ld_all[:, tc_ * NH:(tc_ + 1) * NH],
                                                      zld[:, 0:NH], zld[:, 3 * NH:4 * NH],
                                                      op=ALU.subtract))
                block.vector(ld_vec)
                assert pe.n == pel0 + NT and ac.n == acl0 + 4 * NT
                assert dv.n == dvl0 + 2 * NT
        barrier()
        if phases < 4:
            attn_stack.close()
            return nc

        o_all = attn_stack.enter_context(nc.sbuf_tensor("s_oall", [P, NT * D], BF16))
        scan_sbufs = ExitStack()
        ssb = lambda n_, s_, d_: scan_sbufs.enter_context(nc.sbuf_tensor(n_, s_, d_))
        D_sb2 = [ssb(f"s_D{i}", [P, D], F32) for i in range(2)]
        kT_sb2 = [ssb(f"s_kT{i}", [P, D], BF16) for i in range(2)]
        sh_sb2 = [ssb(f"s_sh{i}", [P, D], BF16) for i in range(2)]
        kd_sb2 = [ssb(f"s_kd{i}", [P, D], BF16) for i in range(2)]
        ktile = [ssb(f"s_kt{i}", [P, D], BF16) for i in range(2)]
        vtile = [ssb(f"s_vt{i}", [P, D], BF16) for i in range(2)]
        qtile = [ssb(f"s_qt{i}", [P, D], BF16) for i in range(2)]
        cum_tok2 = [ssb(f"s_cumtok{i}", [P, NH], F32) for i in range(2)]
        negcum2 = [ssb(f"s_negcum{i}", [P, NH], F32) for i in range(2)]
        cumT2 = [ssb(f"s_cumT{i}", [16, P], F32) for i in range(2)]
        cumrow2 = [ssb(f"s_cumrow{i}", [1, NH * P], F32) for i in range(2)]
        exptot_r2 = [ssb(f"s_exptotr{i}", [1, NH], F32) for i in range(2)]
        exptot_c2 = [ssb(f"s_exptotc{i}", [P, NH], F32) for i in range(2)]

        with scan_sbufs, \
             nc.psum_tensor("s_psA", [P, 512], F32) as ps_A, \
             nc.psum_tensor("s_psB", [P, HB * P], F32) as ps_B, \
             nc.psum_tensor("s_psS", [P, HGRP * P], F32) as ps_sT, \
             nc.psum_tensor("s_psO", [P, HGRP * P], F32) as ps_od, \
             nc.psum_tensor("s_psK", [P, 2048], BF16) as ps_kt:

            def s_init(v):
                sv(v); dv.inc(v.memset(S_sb[:], 0.0))
                sv(v); dv.inc(v.memset(S_bf[:], 0.0))
                sv(v); dv.inc(v.memset(Rtab[0:1, 0:NH], 0.0))
            block.vector(s_init)

            sxs0, sgs0, pes0, dvs0, acs0 = sx.n, sg.n, pe.n, dv.n, ac.n
            skv = [dmasem("skv0"), dmasem("skv1")]

            def kvq_done(m):
                return (skv[m % 2].h, 48 * (m // 2 + 1))
            PTILE = 3 + NCHB + NH + 4 * NG
            DTILE = 6 + 6 * NH
            ATILE = 2 + NH

            def sg_bn(m): return sgs0 + 32 * (m + 1)
            def pe_cum(m): return pes0 + PTILE * m + 1
            def pe_tot(m): return pes0 + PTILE * m + 2
            def pe_etc(m): return pes0 + PTILE * m + 3
            def pe_B(m, ch): return pes0 + PTILE * m + 3 + ch + 1
            def pe_kt(m, h): return pes0 + PTILE * m + 3 + NCHB + h + 1
            def pe_sc(m, g_): return pes0 + PTILE * m + 3 + NCHB + NH + 4 * g_ + 1
            def pe_oi(m, g_): return pes0 + PTILE * m + 3 + NCHB + NH + 4 * g_ + 2
            def pe_or(m, g_): return pes0 + PTILE * m + 3 + NCHB + NH + 4 * g_ + 3
            def pe_dS(m, g_): return pes0 + PTILE * m + 3 + NCHB + NH + 4 * g_ + 4
            def dv_cum(m): return dvs0 + DTILE * m + 3
            def dv_R(m): return dvs0 + DTILE * m + 4
            def dv_etc(m): return dvs0 + DTILE * m + 5
            def dv_kth(m): return dvs0 + DTILE * m + 5 + HGRP
            def dv_kt(m): return dvs0 + DTILE * m + 5 + NH
            def dv_sh(m, g_): return dvs0 + DTILE * m + 5 + NH + 5 * HGRP * g_ + 2 * HGRP
            def dv_ev1(m, g_): return dvs0 + DTILE * m + 5 + NH + 5 * HGRP * g_ + 3 * HGRP
            def dv_oev(m, g_): return dvs0 + DTILE * m + 5 + NH + 5 * HGRP * g_ + 4 * HGRP
            def dv_Su(m, g_): return dvs0 + DTILE * m + 5 + NH + 5 * HGRP * g_ + 5 * HGRP
            def dv_Sbf(m): return dvs0 + DTILE * (m + 1)
            def ac_ec(m): return acs0 + ATILE * m + 1
            def ac_etr(m): return acs0 + ATILE * m + 2
            def ac_D(m, ch): return acs0 + ATILE * m + 2 + (ch + 1) * HB

            def s_sync(s):
                for m in range(NT):
                    if m >= 2:
                        s.wait_ge(pe.h, pe_dS(m - 2, NG - 1))
                        s.wait_ge(dv.h, dv_Su(m - 2, NG - 1))
                    else:
                        s.wait_ge(dv.h, dvs0)
                        s.wait_ge(ac.h, acs0)
                        s.wait_ge(pe.h, pes0)
                    skv[m % 2].inc(s.dma_start(out=ktile[m % 2][:],
                                                in_=k_park[m * P:(m + 1) * P, :]), 16)
                    skv[m % 2].inc(s.dma_start(out=vtile[m % 2][:],
                                               in_=v_park[m * P:(m + 1) * P, :]), 16)
                    skv[m % 2].inc(s.dma_start(
                        out=qtile[m % 2][:].rearrange("p (ec t) -> p ec t", ec=DC),
                        in_=q_park[m * D:(m + 1) * D, :]
                            .rearrange("(ec p) t -> p ec t", p=P)), 16)
            block.sync(s_sync)

            def s_gp(g):
                for m in range(NT):
                    cumT = cumT2[m % 2]
                    cumrow = cumrow2[m % 2]
                    g.wait_ge(dv.h, dv_cum(m))
                    if m >= 2:
                        g.wait_ge(pe.h, pe_B(m - 2, NCHB - 1))
                    sg.inc(g.dma_start(
                        out=bounce[m:m + 1, :].rearrange("a (h t) -> (a h) t", h=NH),
                        in_=cumT[0:NH, :]), 16)
                    g.wait_ge(sg.h, sg.n)
                    sg.inc(g.dma_start(out=cumrow[:], in_=bounce[m:m + 1, :]), 16)
            block.gpsimd(s_gp)
            assert sg.n == sgs0 + 32 * NT

            def s_te(te):
                te.wait_ge(dv.h, dvs0)
                for m in range(NT):
                    cumT = cumT2[m % 2]
                    cumrow = cumrow2[m % 2]
                    exptot_r = exptot_r2[m % 2]
                    kT_sb = kT_sb2[m % 2]
                    sh_sb = sh_sb2[m % 2]
                    kd_sb = kd_sb2[m % 2]
                    if m >= 1:
                        te.wait_ge(dv.h, dv_etc(m - 1))
                        te.wait_ge(ac.h, ac_etr(m - 1))
                    mm = te.matmul(ps_A[:, 0:NH], c_triu[:], ld_all[:, m * NH:(m + 1) * NH],
                                   start=True, stop=True)
                    mm = te.matmul(ps_A[0:NH, 32:32 + P], ld_all[:, m * NH:(m + 1) * NH],
                                   c_triu[:], start=True, stop=True)
                    pe.inc(mm)
                    te.wait_ge(dv.h, dv_cum(m))
                    pe.inc(te.transpose(ps_A[0:1, 160:160 + NH], cumT[0:NH, P - 1:P],
                                        c_identf[0:NH, 0:NH]))
                    te.wait_ge(ac.h, ac_etr(m))
                    pe.inc(te.matmul(ps_A[:, 192:192 + NH], c_ones[:], exptot_r[:],
                                     start=True, stop=True))
                    for ch in range(NCHB):
                        te.wait_ge(sg.h, sg_bn(m))
                        if ch >= 1:
                            te.wait_ge(ac.h, ac_D(m, ch - 1))
                        elif m >= 1:
                            te.wait_ge(ac.h, ac_D(m - 1, NCHB - 1))
                        te.matmul(ps_B[:], c_ones[:],
                                  cumrow[:, ch * HB * P:(ch + 1) * HB * P],
                                  start=True, stop=False)
                        pe.inc(te.matmul(ps_B[:], c_identf[:], c_uxh[:],
                                         start=False, stop=True))
                    te.wait_ge(*kvq_done(m))
                    for h in range(NH):
                        sl = (h % 2) * 1024 + ((h % HGRP) // 2) * P
                        if h == HGRP and NG > 1:
                            te.wait_ge(dv.h, dv_kth(m))
                        elif h == 0 and m >= 1:
                            te.wait_ge(dv.h, dv_kt(m - 1))
                        if h >= 2:
                            te.wait_ge(dv.h, dvs0 + DTILE * m + 5 + (h - 2) + 1)
                        pe.inc(te.transpose(ps_kt[:, sl:sl + P],
                                            ktile[m % 2][:, h * P:(h + 1) * P],
                                            c_identb[:]))
                    for g_ in range(NG):
                        h0 = g_ * HGRP
                        te.wait_ge(dv.h, dv_kth(m) if g_ == 0 else dv_kt(m))
                        if g_ >= 1:
                            te.wait_ge(dv.h, dv_oev(m, g_ - 1))
                        elif m >= 1:
                            te.wait_ge(dv.h, dv_oev(m - 1, NG - 1))
                        for h in range(h0, h0 + HGRP):
                            sl = (h - h0) * P
                            mm = te.matmul(ps_sT[:, sl:sl + P],
                                           kT_sb[:, h * P:(h + 1) * P],
                                           qtile[m % 2][:, h * P:(h + 1) * P],
                                           start=True, stop=True)
                        pe.inc(mm)
                        te.wait_ge(dv.h, dv_sh(m, g_))
                        for h in range(h0, h0 + HGRP):
                            sl = (h - h0) * P
                            mm = te.matmul(ps_sT[:, sl:sl + P],
                                           sh_sb[:, h * P:(h + 1) * P],
                                           vtile[m % 2][:, h * P:(h + 1) * P],
                                           start=True, stop=True)
                        pe.inc(mm)
                        if m >= 1:
                            te.wait_ge(dv.h, dv_Sbf(m - 1))
                        if g_ >= 1:
                            te.wait_ge(dv.h, dv_Su(m, g_ - 1))
                        elif m >= 1:
                            te.wait_ge(dv.h, dv_Su(m - 1, NG - 1))
                        for h in range(h0, h0 + HGRP):
                            sl = (h - h0) * P
                            mm = te.matmul(ps_od[:, sl:sl + P],
                                           qtile[m % 2][:, h * P:(h + 1) * P],
                                           S_bf[:, h * P:(h + 1) * P],
                                           start=True, stop=True)
                        pe.inc(mm)
                        te.wait_ge(dv.h, dv_ev1(m, g_))
                        for h in range(h0, h0 + HGRP):
                            sl = (h - h0) * P
                            mm = te.matmul(ps_od[:, sl:sl + P],
                                           kd_sb[:, h * P:(h + 1) * P],
                                           vtile[m % 2][:, h * P:(h + 1) * P],
                                           start=True, stop=True)
                        pe.inc(mm)
            block.tensor(s_te)
            assert pe.n == pes0 + PTILE * NT

            def s_vec(v):
                for m in range(NT):
                    cum_tok = cum_tok2[m % 2]
                    negcum = negcum2[m % 2]
                    cumT = cumT2[m % 2]
                    exptot_c = exptot_c2[m % 2]
                    D_sb = D_sb2[m % 2]
                    kT_sb = kT_sb2[m % 2]
                    sh_sb = sh_sb2[m % 2]
                    kd_sb = kd_sb2[m % 2]
                    v.wait_ge(pe.h, pe_cum(m))
                    if m >= 2:
                        v.wait_ge(ac.h, ac_D(m - 2, NCHB - 1))
                        v.wait_ge(sg.h, sg_bn(m - 2))
                    sv(v); dv.inc(v.tensor_copy(cum_tok[:], ps_A[:, 0:NH]))
                    sv(v); dv.inc(v.tensor_copy(cumT[0:NH, :], ps_A[0:NH, 32:32 + P]))
                    sv(v); dv.inc(v.tensor_scalar(negcum[:], cum_tok[:], -1.0, None,
                                                  op0=ALU.mult))
                    v.wait_ge(pe.h, pe_tot(m))
                    sv(v); dv.inc(v.tensor_tensor(Rtab[0:1, (m + 1) * NH:(m + 2) * NH],
                                                  Rtab[0:1, m * NH:(m + 1) * NH],
                                                  ps_A[0:1, 160:160 + NH], op=ALU.add))
                    v.wait_ge(pe.h, pe_etc(m))
                    sv(v); dv.inc(v.tensor_copy(exptot_c[:], ps_A[:, 192:192 + NH]))
                    for h in range(NH):
                        v.wait_ge(pe.h, pe_kt(m, h))
                        sl = (h % 2) * 1024 + ((h % HGRP) // 2) * P
                        sv(v); dv.inc(v.tensor_copy(kT_sb[:, h * P:(h + 1) * P],
                                                    ps_kt[:, sl:sl + P]))
                    for g_ in range(NG):
                        h0 = g_ * HGRP
                        v.wait_ge(pe.h, pe_sc(m, g_))
                        for h in range(h0, h0 + HGRP):
                            v.wait_ge(ac.h, ac_D(m, h // HB))
                            sl = (h - h0) * P
                            sv(v); dv.inc(v.tensor_tensor(sh_sb[:, h * P:(h + 1) * P],
                                                          ps_sT[:, sl:sl + P],
                                                          D_sb[:, h * P:(h + 1) * P],
                                                          op=ALU.mult))
                            sv(v); dv.inc(v.tensor_scalar_mul(
                                kd_sb[:, h * P:(h + 1) * P],
                                ktile[m % 2][:, h * P:(h + 1) * P],
                                D_sb[:, h * P + P - 1:h * P + P]))
                        v.wait_ge(pe.h, pe_or(m, g_))
                        v.wait_ge(ac.h, ac_ec(m))
                        for h in range(h0, h0 + HGRP):
                            sl = (h - h0) * P
                            sv(v); dv.inc(v.tensor_scalar_mul(
                                o_all[:, m * D + h * P:m * D + (h + 1) * P],
                                ps_od[:, sl:sl + P],
                                expcum_all[:, m * NH + h:m * NH + h + 1]))
                        v.wait_ge(pe.h, pe_oi(m, g_))
                        for h in range(h0, h0 + HGRP):
                            sl = (h - h0) * P
                            sv(v); dv.inc(v.tensor_tensor(
                                o_all[:, m * D + h * P:m * D + (h + 1) * P],
                                ps_sT[:, sl:sl + P],
                                o_all[:, m * D + h * P:m * D + (h + 1) * P],
                                op=ALU.add))
                        v.wait_ge(pe.h, pe_dS(m, g_))
                        for h in range(h0, h0 + HGRP):
                            sl = (h - h0) * P
                            sv(v); dv.inc(v.scalar_tensor_tensor(
                                S_sb[:, h * P:(h + 1) * P], S_sb[:, h * P:(h + 1) * P],
                                exptot_c[:, h:h + 1], ps_od[:, sl:sl + P],
                                op0=ALU.mult, op1=ALU.add))
                    v.wait_ge(pe.h, pe_or(m, NG - 1))
                    sv(v); dv.inc(v.tensor_copy(S_bf[:], S_sb[:]))
            block.vector(s_vec)
            assert dv.n == dvs0 + DTILE * NT

            def s_sc(s):
                for m in range(NT):
                    cum_tok = cum_tok2[m % 2]
                    negcum = negcum2[m % 2]
                    exptot_r = exptot_r2[m % 2]
                    D_sb = D_sb2[m % 2]
                    s.wait_ge(dv.h, dv_cum(m))
                    ss(s); ac.inc(s.activation(expcum_all[:, m * NH:(m + 1) * NH],
                                               cum_tok[:], AF.Exp))
                    s.wait_ge(dv.h, dv_R(m))
                    if m >= 2:
                        s.wait_ge(pe.h, pe_etc(m - 2))
                    ss(s); ac.inc(s.activation(exptot_r[:], ps_A[0:1, 160:160 + NH],
                                               AF.Exp))
                    for ch in range(NCHB):
                        s.wait_ge(pe.h, pe_B(m, ch))
                        if m >= 2 and ch == 0:
                            s.wait_ge(dv.h, dv_sh(m - 2, NG - 1))
                        for hh in range(HB):
                            h = ch * HB + hh
                            ss(s); ac.inc(s.activation(D_sb[:, h * P:(h + 1) * P],
                                                       ps_B[:, hh * P:(hh + 1) * P],
                                                       AF.Exp, bias=negcum[:, h:h + 1]))
            block.scalar(s_sc)
            assert ac.n == acs0 + ATILE * NT

            gq_sr0 = gq.n

            def s_pub(g):
                g.wait_ge(dv.h, dv_Sbf(NT - 1))
                with g.register("par_s") as par, g.register("want_s") as want, \
                     g.register("seen_s") as seen, g.register("more_s") as more, \
                     g.register("tgt_s") as tgt:
                    g.load(par, parity_sb[0:1, 0:1])
                    g.load(want, nonce_sb[0:1, 0:1])
                    branch_dma(g, par, sg, smbox[0:P, :], smbox[P:2 * P, :], S_sb[:])
                    g.wait_ge(sg.h, sg.n)
                    branch_dma(g, par, sg, flags_s[FLG_S0:FLG_S0 + 1, :],
                               flags_s[FLG_S1:FLG_S1 + 1, :], flagrow[:])
                    g.wait_ge(sg.h, sg.n)
                    g.reg_mov(more, 1)
                    g.reg_mov(tgt, 0)
                    with g.While(more):
                        g.reg_add(tgt, tgt, 16)
                        if solo:
                            g.dma_start(out=flagrd[:],
                                        in_=flags_s[FLG_S1:FLG_S1 + 1, :])\
                                .then_inc(spoll2.h, 16)
                        else:
                            with g.If_eq(par, 0):
                                g.dma_start(out=flagrd[:],
                                            in_=flags_s[FLG_S1:FLG_S1 + 1, :])\
                                    .then_inc(spoll2.h, 16)
                            with g.Else():
                                g.dma_start(out=flagrd[:],
                                            in_=flags_s[FLG_S0:FLG_S0 + 1, :])\
                                    .then_inc(spoll2.h, 16)
                        g.wait_ge(spoll2.h, tgt)
                        g.load(seen, flagrd[0:1, 0:1])
                        g.reg_alu(more, seen, want, ALU.not_equal)
                    if solo:
                        sg.inc(g.dma_start(out=S_sb[:], in_=smbox[0:P, :]), 16)
                    else:
                        branch_dma(g, par, sg, S_sb[:], S_sb[:],
                                   smbox[P:2 * P, :], smbox[0:P, :])
                    g.wait_ge(sg.h, sg.n)
                gq.inc(g.nop())
            block.gpsimd(s_pub)
            gq_srecv = gq.n

            def s_mask(v):
                v.wait_ge(gq.h, gq_srecv)
                sv(v); dv.inc(v.tensor_scalar_mul(SR_bf[:], S_sb[:], ccol(COL_MASK)))
            block.vector(s_mask)
            dv_srbf = dv.n
        barrier()
        if phases < 5:
            attn_stack.close()
            return nc

        with ExitStack() as p5:
            csb = lambda n_, s_, d_: p5.enter_context(nc.sbuf_tensor(n_, s_, d_))
            wo_bf = csb("c_wobf", [P, DC * D], BF16)
            normw_row = csb("c_normwrow", [1, D], F32)
            normw_b = csb("c_normwb", [P, D], BF16)
            o_f32 = csb("c_of32", [P, D], F32)
            cstat = csb("c_cstat", [P, 4], F32)
            expR = csb("c_expR", [1, NH], F32)
            colsc = csb("c_colsc", [P, NH], F32)
            on_bf = csb("c_onbf", [P, D], BF16)
            ont = csb("c_ont", [P, D], BF16)
            x1t1 = csb("c_x1t", [P, D], F32)
            x1t = [x1t1, x1t1]
            qt5 = [csb(f"c_qt5{i}", [P, D], BF16) for i in range(2)]
            x2t1 = csb("c_x2t", [P, D], F32)
            x2t = [x2t1, x2t1]


            with nc.psum_tensor("c_psc", [P, D], F32) as ps_c, \
                 nc.psum_tensor("c_psw", [P, 1024], F32) as ps_w, \
                 nc.psum_tensor("c_pst", [P, 2048], BF16) as ps_t5:

                sg5a = sg.n

                def c_gp0(g):
                    sg.inc(g.dma_start(
                        out=wo_bf[:].rearrange("p (fc e) -> p fc e", fc=DC),
                        in_=wo_d.rearrange("(fc p) e -> p fc e", p=P)), 16)
                block.gpsimd(c_gp0)
                sx5a = sx.n

                def c_sync0(s):
                    sx.inc(s.dma_start(out=normw_row[:], in_=normw_d), 16)
                block.sync(c_sync0)

                pe5a = pe.n

                dv5a = dv.n

                def c_te0(te):
                    te.wait_ge(sx.h, sx5a + 16)
                    for j in range(NDS):
                        if j >= 1:
                            te.wait_ge(dv.h, dv5a + j)
                        pe.inc(te.matmul(ps_w[:, 0:512], c_ones[:],
                                         normw_row[0:1, j * 512:(j + 1) * 512],
                                         start=True, stop=True))
                block.tensor(c_te0)

                def c_vec0(v):
                    for j in range(NDS):
                        v.wait_ge(pe.h, pe5a + j + 1)
                        sv(v); dv.inc(v.tensor_copy(normw_b[:, j * 512:(j + 1) * 512],
                                                    ps_w[:, 0:512]))
                block.vector(c_vec0)
                dv_nb = dv.n

                sx50, sg50, pe50, dv50, ac50 = sx.n, sg.n, pe.n, dv.n, ac.n
                s15 = [dmasem("s150"), dmasem("s151")]
                sx1 = dmasem("sx1")
                s2one = dmasem("s2one")
                s2o0 = s2one.n
                sx10 = sx1.n

                def xq5_done(m):
                    return (s15[m % 2].h, 16 * (m // 2 + 1))

                def x1_done(m):
                    return (sx1.h, sx10 + 16 * (m + 1))

                def x2_free(m):
                    return (s2one.h, s2o0 + 16 * m)
                PT5 = 1 + 1 + DC + NDS
                DT5 = 2 + NH + 1 + 1 + DC + NDS
                AT5 = 3


                def ac_exR(m): return ac50 + AT5 * m + 1
                def ac_sq(m): return ac50 + AT5 * m + 2
                def ac_srt(m): return ac50 + AT5 * m + 3
                def pe_msk(m): return pe50 + PT5 * m + 1
                def pe_corr(m): return pe50 + PT5 * m + 2
                def pe_tr5(m, dc_): return pe50 + PT5 * m + 2 + dc_ + 1
                def pe_wo(m, ds_): return pe50 + PT5 * m + 2 + DC + ds_ + 1
                def dv_colsc(m): return dv50 + DT5 * m + 1
                def dv_of(m): return dv50 + DT5 * m + 2
                def dv_stt(m): return dv50 + DT5 * m + 2 + NH
                def dv_rstd(m): return dv50 + DT5 * m + 3 + NH
                def dv_on(m): return dv50 + DT5 * m + 4 + NH
                def dv_trev(m, dc_): return dv50 + DT5 * m + 4 + NH + dc_ + 1
                def dv_trall(m): return dv50 + DT5 * m + 4 + NH + DC
                def dv_woev(m, ds_): return dv50 + DT5 * m + 4 + NH + DC + ds_ + 1
                def dv_x2(m): return dv50 + DT5 * (m + 1)

                def c_sync(s):
                    for m in range(NT):
                        if m >= 1:
                            s.wait_ge(dv.h, dv_woev(m - 1, NDS - 1))
                        sx1.inc(s.dma_start(out=x1t[m % 2][:],
                                            in_=x1_park[m * P:(m + 1) * P, :]), 16)
                        if m >= 2:
                            s.wait_ge(pe.h, pe_corr(m - 2))
                        s15[m % 2].inc(s.dma_start(
                            out=qt5[m % 2][:].rearrange("p (ec t) -> p ec t", ec=DC),
                            in_=q_park[m * D:(m + 1) * D, :]
                                .rearrange("(ec p) t -> p ec t", p=P)), 16)
                block.sync(c_sync)

                def c_sc(s):
                    for m in range(NT):
                        if m >= 1:
                            s.wait_ge(pe.h, pe_msk(m - 1))
                        ss(s); ac.inc(s.activation(expR[:], Rtab[0:1, m * NH:(m + 1) * NH],
                                                   AF.Exp))
                        s.wait_ge(dv.h, dv_stt(m))
                        if m >= 1:
                            s.wait_ge(pe.h, pe_wo(m - 1, NDS - 1))
                        ss(s); ac.inc(s.activation(ont[:], o_f32[:], AF.Square,
                                                   accum_out=cstat[:, 0:1]))
                        ss(s); ac.inc(s.activation(cstat[:, 1:2], cstat[:, 0:1], AF.Sqrt,
                                                   scale=ccol(COL_INVD),
                                                   bias=ccol(COL_RMSEPS)))
                block.scalar(c_sc)

                def c_te(te):
                    for m in range(NT):
                        te.wait_ge(ac.h, ac_exR(m))
                        if m >= 1:
                            te.wait_ge(dv.h, dv_colsc(m - 1))
                        else:
                            te.wait_ge(dv.h, dv_nb)
                        if m >= 1:
                            te.wait_ge(dv.h, dv_woev(m - 1, max(0, NDS - 2)))
                        pe.inc(te.matmul(ps_w[:, 0:NH], c_mask[:], expR[:],
                                         start=True, stop=True))
                        te.wait_ge(*xq5_done(m))
                        te.wait_ge(dv.h, dv_srbf)
                        if m >= 1:
                            te.wait_ge(dv.h, dv_stt(m - 1))
                        for h in range(NH):
                            mm = te.matmul(ps_c[:, h * P:(h + 1) * P],
                                           qt5[m % 2][:, h * P:(h + 1) * P],
                                           SR_bf[:, h * P:(h + 1) * P],
                                           start=True, stop=True)
                        pe.inc(mm)
                        te.wait_ge(dv.h, dv_on(m))
                        for dc_ in range(DC):
                            if dc_ >= 2:
                                te.wait_ge(dv.h, dv_trev(m, dc_ - 2))
                            elif m >= 1:
                                te.wait_ge(dv.h, dv_trall(m - 1))
                            sl5 = (dc_ % 2) * 1024
                            pe.inc(te.transpose(ps_t5[:, sl5:sl5 + P],
                                                on_bf[:, dc_ * P:(dc_ + 1) * P],
                                                c_identb[:]))
                        te.wait_ge(sg.h, sg5a + 16)
                        te.wait_ge(dv.h, dv_trall(m))
                        for ds_ in range(NDS):
                            bank = ds_ % 2
                            if ds_ == 0:
                                te.wait_ge(dv.h, dv_colsc(m))
                            if ds_ >= 2:
                                te.wait_ge(dv.h, dv_woev(m, ds_ - 2))
                            elif m >= 1:
                                te.wait_ge(dv.h, dv_woev(m - 1, NDS - 2 + ds_))
                            for fc in range(DC):
                                mm = te.matmul(
                                    ps_w[:, bank * 512:(bank + 1) * 512],
                                    ont[:, fc * P:(fc + 1) * P],
                                    wo_bf[:, fc * D + ds_ * 512:fc * D + (ds_ + 1) * 512],
                                    start=(fc == 0), stop=(fc == DC - 1))
                            pe.inc(mm)
                block.tensor(c_te)
                assert pe.n == pe50 + PT5 * NT

                def c_vec(v):
                    for m in range(NT):
                        v.wait_ge(pe.h, pe_msk(m))
                        sv(v); dv.inc(v.tensor_tensor(colsc[:],
                                                      expcum_all[:, m * NH:(m + 1) * NH],
                                                      ps_w[:, 0:NH], op=ALU.mult))
                        sv(v); dv.inc(v.tensor_copy(o_f32[:], o_all[:, m * D:(m + 1) * D]))
                        v.wait_ge(pe.h, pe_corr(m))
                        for h in range(NH):
                            sv(v); dv.inc(v.scalar_tensor_tensor(
                                o_f32[:, h * P:(h + 1) * P], ps_c[:, h * P:(h + 1) * P],
                                colsc[:, h:h + 1], o_f32[:, h * P:(h + 1) * P],
                                op0=ALU.mult, op1=ALU.add))
                        v.wait_ge(ac.h, ac_srt(m))
                        sv(v); dv.inc(v.reciprocal(cstat[:, 2:3], cstat[:, 1:2]))
                        v.wait_ge(dv.h, dv_nb)
                        sv(v); dv.inc(v.scalar_tensor_tensor(on_bf[:], o_f32[:],
                                                             cstat[:, 2:3], normw_b[:],
                                                             op0=ALU.mult, op1=ALU.mult))
                        for dc_ in range(DC):
                            v.wait_ge(pe.h, pe_tr5(m, dc_))
                            sl5 = (dc_ % 2) * 1024
                            sv(v); dv.inc(v.tensor_copy(ont[:, dc_ * P:(dc_ + 1) * P],
                                                        ps_t5[:, sl5:sl5 + P]))
                        v.wait_ge(*x1_done(m))
                        for ds_ in range(NDS):
                            bank = ds_ % 2
                            v.wait_ge(pe.h, pe_wo(m, ds_))
                            if m >= 1:
                                v.wait_ge(*x2_free(m))
                            sv(v); dv.inc(v.tensor_tensor(
                                x2t[m % 2][:, ds_ * 512:(ds_ + 1) * 512],
                                ps_w[:, bank * 512:(bank + 1) * 512],
                                x1t[m % 2][:, ds_ * 512:(ds_ + 1) * 512], op=ALU.add))
                block.vector(c_vec)
                assert dv.n == dv50 + DT5 * NT

                def c_store(g):
                    for m in range(NT):
                        g.wait_ge(dv.h, dv_x2(m))
                        s2one.inc(g.dma_start(out=x2_park[m * P:(m + 1) * P, :],
                                              in_=x2t[m % 2][:]), 16)
                    g.wait_ge(s2one.h, s2one.n)
                block.gpsimd(c_store)
        attn_stack.close()
        barrier()
        if phases < 6:
            return nc

        with ExitStack() as f2stk:
            ps_f2 = [f2stk.enter_context(nc.psum_tensor(f"f2_ps{i}", [P, 512], F32))
                     for i in range(max(4, NGD * NDS))]
            ffn2 = make_ffn("f2", x2_park, y_out, False, f2stk, ps_f2)
            ffn2["prologue"](fence=[(phb.h, N_ENG * phase_i[0])])
            for half in range(c.NHALF):
                ffn2["template"](half)

        def fin_sync(s):
            s.wait_ge(sx.h, sx.n)
        block.sync(fin_sync)

        def fin_gp(g):
            g.wait_ge(sg.h, sg.n)
        block.gpsimd(fin_gp)

    return nc



def _make_consts(cfg, par):
    a = np.zeros((P, 8), np.float32)
    a[:, 0] = LN_EPS
    a[:, 1] = RMS_EPS
    a[:, 2] = 1.0
    a[:, 3] = -1.0
    a[:, 4] = 1.0 / cfg.D
    a[:, 5] = 0.5
    a[:, 6] = 0.0
    a[:, 7] = 1.0 if par == 1 else 0.0
    return a


def make_in_maps(cfg, inputs, nonce_val, n_cores=8):
    c = cfg
    x = np.asarray(inputs["L0_attn_x"])
    Wq = np.asarray(inputs["L0_attn_Wq"], np.float32)
    Wk = np.asarray(inputs["L0_attn_Wk"], np.float32)
    Wv = np.asarray(inputs["L0_attn_Wv"], np.float32)
    Wf = np.asarray(inputs["L0_attn_Wf"], np.float32)
    Wo = np.asarray(inputs["L0_attn_Wo"], np.float32)
    nw = np.asarray(inputs["L0_attn_norm_w"], np.float32)
    lut = np.asarray(inputs["L0_ffn_lut"], np.float32)
    gidx = np.asarray(inputs["L0_ffn_gate_idx"], np.int32)
    uidx = np.asarray(inputs["L0_ffn_up_idx"], np.int32)
    didx = np.asarray(inputs["L0_ffn_down_idx"], np.int32)
    D = c.D
    OCHh = c.OCH
    NRBF = (OCHh + 1) // 2
    IHE = NRBF * P
    gT = np.ascontiguousarray(gidx.T)
    uT = np.ascontiguousarray(uidx.T)
    dT = np.ascontiguousarray(didx.T)

    def blocked(mT):
        return np.ascontiguousarray(
            mT.reshape(D, c.OCH, P).transpose(1, 0, 2).reshape(c.OCH * D, P))

    gB, uB = blocked(gT), blocked(uT)
    HB_ = 4
    triu = np.triu(np.ones((P, P))).astype(np.float32)
    identb = np.eye(P).astype(ml_dtypes.bfloat16)
    identf = np.eye(P).astype(np.float32)
    uxh = np.tile(np.tril(np.ones((P, P), np.float32), -1) * BIGNEG, (1, HB_))
    ones_row = np.ones((1, P), np.float32)

    in_maps = []
    for core in range(n_cores):
        b, half = core // 2, core % 2
        par = core % 2
        in_maps.append({
            "x_in": np.ascontiguousarray(x[b, half * c.TC:(half + 1) * c.TC, :],
                                         dtype=np.float32),
            "wq": Wq, "wk": Wk, "wv": Wv, "wo": Wo, "wf": Wf,
            "normw": np.ascontiguousarray(nw.reshape(1, D)),
            "lutrow": np.ascontiguousarray(lut.reshape(1, 16)),
            "mata_idx": (gB if par == 0 else uB),
            "matb_idx": (np.ascontiguousarray(dT[0:IHE, :]) if par == 0 else
                         np.ascontiguousarray(
                             np.vstack([dT[IHE:], dT[-(2 * IHE - c.INTER):]])
                             if 2 * IHE > c.INTER else dT[IHE:2 * IHE])),
            "triu": triu, "identb": identb, "identf": identf, "uxh": uxh,
            "onesrow": ones_row,
            "maskrow": np.full((1, P), 1.0 if par == 1 else 0.0, np.float32),
            "consts": _make_consts(c, par),
            "parity": np.array([[par]], np.int32),
            "nonce": np.array([[nonce_val]], np.int32),
        })
    return in_maps


_nc_cache = {}


def kernel(L0_attn_x, L0_attn_Wq, L0_attn_Wk, L0_attn_Wv, L0_attn_Wf, L0_attn_Wo,
           L0_attn_norm_w, L0_ffn_lut, L0_ffn_gate_idx, L0_ffn_up_idx, L0_ffn_down_idx):
    cfg = Cfg()
    nonce_val = int(np.random.randint(1, 2 ** 24))
    inputs = dict(L0_attn_x=L0_attn_x, L0_attn_Wq=L0_attn_Wq, L0_attn_Wk=L0_attn_Wk,
                  L0_attn_Wv=L0_attn_Wv, L0_attn_Wf=L0_attn_Wf, L0_attn_Wo=L0_attn_Wo,
                  L0_attn_norm_w=L0_attn_norm_w, L0_ffn_lut=L0_ffn_lut,
                  L0_ffn_gate_idx=L0_ffn_gate_idx, L0_ffn_up_idx=L0_ffn_up_idx,
                  L0_ffn_down_idx=L0_ffn_down_idx)
    in_maps = make_in_maps(cfg, inputs, nonce_val)
    if "nc" not in _nc_cache:
        _nc_cache["nc"] = build_program(cfg)
    nc = _nc_cache["nc"]
    res = run_bass_kernel_spmd(nc, in_maps, core_ids=list(range(8)))
    B, N, D = L0_attn_x.shape
    y = np.empty((B, N, D), np.float32)
    for core in range(8):
        b, half = core // 2, core % 2
        y[b, half * cfg.TC:(half + 1) * cfg.TC, :] = res.results[core]["y_out"]
    return y
